# version 31
# baseline (speedup 1.0000x reference)
import numpy as np
from contextlib import ExitStack

import concourse.bass as bass
import concourse.mybir as mybir
from concourse.bass_utils import run_bass_kernel_spmd

F32 = mybir.dt.float32
BF16 = mybir.dt.bfloat16
I32 = mybir.dt.int32
AF = mybir.ActivationFunctionType
ALU = mybir.AluOpType

NCORES = 8
S_FULL = 16384
D = 1024
T = S_FULL // NCORES
NTC = T // 128
H_PAD = 2816
NHC = H_PAD // 128
LN_EPS = 1e-5
ATTN_EPS = 1e-6
MAGIC = 12582912.0
DEBUG = False

BA_COLS = 63488
WQOM = 0
PCKI = 8192
ISCR = 12288
WQTS = 16384
XNF = 0
WKV = 16384
WQ = 16384
KPH = 32768
VH = 40960
QP = 32768
XSTG_A = 49152
ONESB = 51200
KVB = 0
WRD = 8256
XQF = 12352
WQTB = 16384
XN1F = 16384
WRG = 24576
XSTG_B = 28672
SSCR_A = 51208
SSCR_B = 14408


def qp_base(half):
    return QP + (1 - half) * 8192


def hact_base(half):
    return 40960 if half == 0 else 32768


XST = 0
OST = 2048
SCR0 = 4096
SCR1 = 5120
SCR2 = 6144
BIAS = 7168
GB = 9216
ZEROS = 11264
SM = 11776
SS1 = SM
SS2 = SM + 16
QQ1 = SM + 32
QQ2 = SM + 48
MX1 = SM + 64
MX2 = SM + 80
MN1 = SM + 96
MN2 = SM + 112
MU_A = SM + 128
R_A = SM + 144
RS_A = SM + 160
WSRAS = SM + 176
ASC = SM + 192
TMP2 = SM + 208
TMP3 = SM + 224
BQ = SM + 240
BG = SM + 248
BU = SM + 270
WS = SM + 292
VW = SM + 296
NW0 = SM + 312
DEN = SM + 328
RD = SM + 329
TMP = SM + 332
BS1 = SM + 344
BQ1 = SM + 376
MU2 = SM + 408
R2 = SM + 416
RF = SM + 512
FA_COLS = RF + 8256


def build_nc():
    nc = bass.Bass("TRN2", target_bir_lowering=False, debug=False)

    x_d = nc.dram_tensor("x", [T, D], F32, kind="ExternalInput").ap()
    wqkv_d = nc.dram_tensor("wqkv", [D, 3 * D], F32, kind="ExternalInput").ap()
    wg_d = nc.dram_tensor("wg", [D, H_PAD], F32, kind="ExternalInput").ap()
    wu_d = nc.dram_tensor("wu", [D, H_PAD], F32, kind="ExternalInput").ap()
    wd_d = nc.dram_tensor("wd", [H_PAD, D], F32, kind="ExternalInput").ap()
    packed_d = nc.dram_tensor("packed", [256, 1024], I32, kind="ExternalInput").ap()
    bq_d = nc.dram_tensor("bq_pp", [128, 8], F32, kind="ExternalInput").ap()
    bkv_d = nc.dram_tensor("bkv_bc", [128, 2048], F32, kind="ExternalInput").ap()
    bg_d = nc.dram_tensor("bg_pp", [128, 22], F32, kind="ExternalInput").ap()
    bu_d = nc.dram_tensor("bu_pp", [128, 22], F32, kind="ExternalInput").ap()
    bd_d = nc.dram_tensor("bd_bc", [128, 1024], F32, kind="ExternalInput").ap()
    bffn_d = nc.dram_tensor("bffn_bc", [128, 1024], F32, kind="ExternalInput").ap()
    g1_d = nc.dram_tensor("g1_bc", [128, 1024], F32, kind="ExternalInput").ap()
    b1_d = nc.dram_tensor("b1_bc", [128, 1024], F32, kind="ExternalInput").ap()
    g2_d = nc.dram_tensor("g2_bc", [128, 1024], F32, kind="ExternalInput").ap()
    b2_d = nc.dram_tensor("b2_bc", [128, 1024], F32, kind="ExternalInput").ap()
    ws_d = nc.dram_tensor("wscale", [128, 1], F32, kind="ExternalInput").ap()
    out_d = nc.dram_tensor("out", [T, D], F32, kind="ExternalOutput").ap()

    dbg = {}
    if DEBUG:
        dbg["xnf"] = nc.dram_tensor("dbg_xnf", [128, 16384], BF16, kind="ExternalOutput").ap()
        dbg["kpv"] = nc.dram_tensor("dbg_kpv", [128, 16384], BF16, kind="ExternalOutput").ap()
        dbg["kvpre"] = nc.dram_tensor("dbg_kvpre", [128, 8256], F32, kind="ExternalOutput").ap()
        dbg["kvpost"] = nc.dram_tensor("dbg_kvpost", [128, 8256], F32, kind="ExternalOutput").ap()
        dbg["qp"] = nc.dram_tensor("dbg_qp", [128, 16384], BF16, kind="ExternalOutput").ap()
        dbg["x1"] = nc.dram_tensor("dbg_x1", [128, 8192], F32, kind="ExternalOutput").ap()
        dbg["xn1"] = nc.dram_tensor("dbg_xn1", [128, 8192], BF16, kind="ExternalOutput").ap()
        dbg["ba"] = nc.dram_tensor("dbg_ba", [128, BA_COLS], BF16, kind="ExternalOutput").ap()
        dbg["fa"] = nc.dram_tensor("dbg_fa", [128, FA_COLS], F32, kind="ExternalOutput").ap()
    cc_in = nc.dram_tensor("cc_in", [128, 8256], BF16)
    cc_out = nc.dram_tensor("cc_out", [128, 8256], BF16)
    wqt_dram = nc.dram_tensor("wqt_dram", [128, 8192], BF16)
    wgu_dram = nc.dram_tensor("wgu_dram", [128, 44 * 1024], BF16)
    wdb_dram = nc.dram_tensor("wdb_dram", [128, 22 * 1024], BF16)
    wq_dram = nc.dram_tensor("wq_dram", [128, 8192], BF16)

    with ExitStack() as ctx:
        e = ctx.enter_context
        BA = e(nc.sbuf_tensor("BA", [128, BA_COLS], BF16))
        FA = e(nc.sbuf_tensor("FA", [128, FA_COLS], F32))
        PS = [e(nc.psum_tensor("ps%d" % i, [128, 512], F32)) for i in range(8)]
        sem_names = ["sd", "gd", "te", "ve", "se", "cc", "dx0", "dx1",
                     "std0", "std1", "std2", "std3", "stq0", "stq1",
                     "stq2", "stq3", "stw", "wr0", "wr1",
                     "wdr0", "wdr1", "wdr2", "wdr3", "do0", "do1",
                     "cl0", "cl1", "cl2", "cl3", "cs0", "cs1", "cs2", "cs3"]
        sems = {n: e(nc.semaphore(n)) for n in sem_names}
        sd = sems["sd"]
        gd = sems["gd"]
        block = e(nc.Block())
        cnt = {k: 0 for k in sems}
        script = []

        def op(eng, fn, inc=None, n=1):
            tok = None
            if inc is not None:
                cnt[inc] += n
                tok = (inc, cnt[inc])

                def wrapped(E, fn=fn, inc=inc, n=n):
                    fn(E).then_inc(sems[inc], n)
                script.append((eng, wrapped))
            else:
                script.append((eng, lambda E, fn=fn: fn(E)))
            return tok

        def wait(eng, tok):
            if tok is None:
                return
            s, v = tok
            script.append((eng, lambda E, s=s, v=v: E.wait_ge(sems[s], v)))

        def op_sd(fn):
            tok = op("sync", fn, inc="sd", n=16)
            wait("sync", tok)
            return tok

        def ba(a, b):
            return BA[:, a:b]

        def fa(a, b):
            return FA[:, a:b]

        def bai32(a, b):
            return BA[:, a:b].bitcast(I32)

        def ssums(src_ap, sb_col, qb_col, scr):
            op("scalar", lambda E, src_ap=src_ap, sb_col=sb_col, scr=scr: E.activation(
                scr, src_ap[:, 0:512], AF.Copy, accum_out=fa(sb_col, sb_col + 1)))
            op("scalar", lambda E, src_ap=src_ap, sb_col=sb_col, scr=scr: E.activation(
                scr, src_ap[:, 512:1024], AF.Copy, accum_out=fa(sb_col + 16, sb_col + 17)))
            op("scalar", lambda E, src_ap=src_ap, qb_col=qb_col, scr=scr: E.activation(
                scr, src_ap[:, 0:512], AF.Square, accum_out=fa(qb_col, qb_col + 1)))
            return op("scalar", lambda E, src_ap=src_ap, qb_col=qb_col, scr=scr: E.activation(
                scr, src_ap[:, 512:1024], AF.Square, accum_out=fa(qb_col + 16, qb_col + 17)), inc="se")

        def ln_batch(sb, qb, mu_base, var_base, r_base, n, bneg_base=None, a_tail=None):
            W = 64
            mu = fa(SCR0, SCR0 + W)
            var = fa(SCR0 + 64, SCR0 + 64 + W)
            rr = fa(SCR0 + 128, SCR0 + 128 + W)
            vw = fa(SCR0 + 192, SCR0 + 192 + W)
            nw = fa(SCR0 + 256, SCR0 + 256 + W)
            t2 = fa(SCR0 + 320, SCR0 + 320 + W)
            t3 = fa(SCR0 + 384, SCR0 + 384 + W)
            mx = fa(SCR0 + 448, SCR0 + 448 + W)
            mn = fa(SCR0 + 512, SCR0 + 512 + W)
            asc_ = fa(SCR0 + 576, SCR0 + 576 + W)
            bn = fa(SCR0 + 640, SCR0 + 640 + W)
            op("vector", lambda E, a=mu[:, 0:n], b=fa(sb, sb + n), c=fa(sb + 16, sb + 16 + n): E.tensor_tensor(a, b, c, ALU.add))
            op("vector", lambda E, a=var[:, 0:n], b=fa(qb, qb + n), c=fa(qb + 16, qb + 16 + n): E.tensor_tensor(a, b, c, ALU.add))
            if a_tail is not None:
                off = a_tail
                op("vector", lambda E, a=mx[:, 0:n], b=fa(MX1 + off, MX1 + off + n), c=fa(MX2 + off, MX2 + off + n): E.tensor_tensor(a, b, c, ALU.max))
                op("vector", lambda E, a=mn[:, 0:n], b=fa(MN1 + off, MN1 + off + n), c=fa(MN2 + off, MN2 + off + n): E.tensor_tensor(a, b, c, ALU.min))
            vdrain()
            op("vector", lambda E, a=mu: E.tensor_scalar(a, a, 1.0 / 1024.0, None, ALU.mult))
            op("vector", lambda E, a=var: E.tensor_scalar(a, a, 1.0 / 1024.0, None, ALU.mult))
            op("vector", lambda E, a=t2, b=mu: E.tensor_tensor(a, b, b, ALU.mult))
            op("vector", lambda E, a=var, b=t2: E.tensor_tensor(a, a, b, ALU.subtract))
            op("vector", lambda E, a=vw, b=var: E.tensor_scalar(a, b, LN_EPS, None, ALU.add))
            op("vector", lambda E, a=rr, b=vw: E.tensor_scalar(a.bitcast(I32), b.bitcast(I32), 1, None, ALU.logical_shift_right))
            op("vector", lambda E, a=rr: E.tensor_scalar(a.bitcast(I32), a.bitcast(I32), 0x5F3759DF, None, ALU.subtract))
            op("vector", lambda E, a=rr: E.tensor_scalar(a.bitcast(I32), a.bitcast(I32), -1, None, ALU.mult))
            for _ in range(3):
                op("vector", lambda E, a=nw, b=rr: E.tensor_tensor(a, b, b, ALU.mult))
                op("vector", lambda E, a=nw, b=vw: E.tensor_tensor(a, a, b, ALU.mult))
                op("vector", lambda E, a=nw: E.tensor_scalar(a, a, -0.5, 1.5, ALU.mult, ALU.add))
                op("vector", lambda E, a=rr, b=nw: E.tensor_tensor(a, a, b, ALU.mult))
            if bneg_base is not None:
                op("vector", lambda E, a=bn, b=mu, c=rr: E.tensor_tensor(a, b, c, ALU.mult))
                op("vector", lambda E, a=bn: E.tensor_scalar(a, a, -1.0, None, ALU.mult))
            if a_tail is not None:
                op("vector", lambda E, a=mx, b=mu: E.tensor_tensor(a, a, b, ALU.subtract))
                op("vector", lambda E, a=mn, b=mu: E.tensor_tensor(a, b, a, ALU.subtract))
                op("vector", lambda E, a=t3, b=mx, c=mn: E.tensor_tensor(a, b, c, ALU.max))
                op("vector", lambda E, a=t3, b=rr: E.tensor_tensor(a, a, b, ALU.mult))
                op("vector", lambda E, a=t3: E.tensor_scalar(a, a, 1e-5, None, ALU.max))
                op("vector", lambda E, a=t2, b=t3: E.reciprocal(a, b))
                op("vector", lambda E, a=asc_, b=t2: E.tensor_scalar(a, b, 127.0, None, ALU.mult))
                op("vector", lambda E, a=t3, b=asc_: E.reciprocal(a, b))
            op("vector", lambda E, a=fa(mu_base, mu_base + n), b=mu[:, 0:n]: E.tensor_copy(a, b))
            op("vector", lambda E, a=fa(r_base, r_base + n), b=rr[:, 0:n]: E.tensor_copy(a, b))
            if bneg_base is not None:
                op("vector", lambda E, a=fa(bneg_base, bneg_base + n), b=bn[:, 0:n]: E.tensor_copy(a, b))
            if a_tail is not None:
                off = a_tail
                op("vector", lambda E, a=fa(WSRAS + off, WSRAS + off + n), b=t3[:, 0:n]: E.tensor_scalar(a, b, fa(WS, WS + 1), None, ALU.mult))
                op("vector", lambda E, a=fa(RS_A + off, RS_A + off + n), b=rr[:, 0:n], c=asc_[:, 0:n]: E.tensor_tensor(a, b, c, ALU.mult))
            vdrain()

        def vevt():
            return op("vector", lambda E: E.drain(), inc="ve")

        def vdrain():
            op("vector", lambda E: E.drain())

        def rsqrt_batch(var_ap, rr, w):
            nw_a, nw_b = w
            op("vector", lambda E, nw_a=nw_a, var_ap=var_ap: E.tensor_scalar(nw_a, var_ap, LN_EPS, None, ALU.add))
            vdrain()
            op("vector", lambda E, rr=rr, nw_a=nw_a: E.tensor_scalar(rr.bitcast(I32), nw_a.bitcast(I32), 1, None, ALU.logical_shift_right))
            vdrain()
            op("vector", lambda E, rr=rr: E.tensor_scalar(rr.bitcast(I32), rr.bitcast(I32), 0x5F3759DF, None, ALU.subtract))
            vdrain()
            op("vector", lambda E, rr=rr: E.tensor_scalar(rr.bitcast(I32), rr.bitcast(I32), -1, None, ALU.mult))
            vdrain()
            for _ in range(3):
                op("vector", lambda E, rr=rr, nw_b=nw_b: E.tensor_tensor(nw_b, rr, rr, ALU.mult))
                vdrain()
                op("vector", lambda E, nw_b=nw_b, nw_a=nw_a: E.tensor_tensor(nw_b, nw_b, nw_a, ALU.mult))
                vdrain()
                op("vector", lambda E, nw_b=nw_b: E.tensor_scalar(nw_b, nw_b, -0.5, 1.5, ALU.mult, ALU.add))
                vdrain()
                op("vector", lambda E, rr=rr, nw_b=nw_b: E.tensor_tensor(rr, rr, nw_b, ALU.mult))
                vdrain()

        def sums_emit(src, sb, qb, sqb, col):
            sq = fa(sqb, sqb + 1024)
            op("vector", lambda E, sq=sq, src=src: E.tensor_tensor(sq, src, src, ALU.mult))
            op("vector", lambda E, src=src, sb=sb, col=col: E.reduce_sum(fa(sb + col, sb + col + 1), src[:, 0:512], axis=mybir.AxisListType.X))
            op("vector", lambda E, src=src, sb=sb, col=col: E.reduce_sum(fa(sb + 16 + col, sb + 16 + col + 1), src[:, 512:1024], axis=mybir.AxisListType.X))
            op("vector", lambda E, sq=sq, qb=qb, col=col: E.reduce_sum(fa(qb + col, qb + col + 1), sq[:, 0:512], axis=mybir.AxisListType.X))
            return op("vector", lambda E, sq=sq, qb=qb, col=col: E.reduce_sum(fa(qb + 16 + col, qb + 16 + col + 1), sq[:, 512:1024], axis=mybir.AxisListType.X), inc="ve")

        def meanvar_batch(sb, qb, mu_ap, var_ap, n):
            s1 = fa(sb, sb + n); s2 = fa(sb + 16, sb + 16 + n)
            q1 = fa(qb, qb + n); q2 = fa(qb + 16, qb + 16 + n)
            t2 = fa(TMP2, TMP2 + n)
            vdrain()
            op("vector", lambda E, mu_ap=mu_ap, s1=s1, s2=s2: E.tensor_tensor(mu_ap, s1, s2, ALU.add))
            op("vector", lambda E, var_ap=var_ap, q1=q1, q2=q2: E.tensor_tensor(var_ap, q1, q2, ALU.add))
            vdrain()
            op("vector", lambda E, mu_ap=mu_ap: E.tensor_scalar(mu_ap, mu_ap, 1.0 / 1024.0, None, ALU.mult))
            op("vector", lambda E, var_ap=var_ap: E.tensor_scalar(var_ap, var_ap, 1.0 / 1024.0, None, ALU.mult))
            vdrain()
            op("vector", lambda E, t2=t2, mu_ap=mu_ap: E.tensor_tensor(t2, mu_ap, mu_ap, ALU.mult))
            vdrain()
            op("vector", lambda E, var_ap=var_ap, t2=t2: E.tensor_tensor(var_ap, var_ap, t2, ALU.subtract))
            vdrain()

        t_zeros = op("vector", lambda E: E.memset(fa(ZEROS, ZEROS + 512), 0.0), inc="ve")
        t_ones = op("vector", lambda E: E.memset(ba(ONESB, ONESB + 8), 1.0), inc="ve")

        op_sd(lambda E: E.dma_start(out=fa(BQ, BQ + 8), in_=bq_d[:, :]))
        op_sd(lambda E: E.dma_start(out=fa(BG, BG + 22), in_=bg_d[:, :]))
        op_sd(lambda E: E.dma_start(out=fa(BU, BU + 22), in_=bu_d[:, :]))
        t_bkv = op_sd(lambda E: E.dma_start(out=fa(BIAS, BIAS + 2048), in_=bkv_d[:, :]))
        t_wsb = op_sd(lambda E: E.dma_start(out=fa(WS, WS + 1), in_=ws_d[:, :]))

        op("gpsimd", lambda E: E.dma_start(out=bai32(PCKI, PCKI + 2048), in_=packed_d[0:128, :]), inc="gd", n=16)
        t_pck = op("gpsimd", lambda E: E.dma_start(out=bai32(PCKI + 2048, PCKI + 4096), in_=packed_d[128:256, :]), inc="gd", n=16)
        wait("vector", t_pck)
        unp_tok = [None] * 4
        for s in range(4):
            op("vector", lambda E, s=s: E.tensor_scalar(bai32(ISCR, ISCR + 4096), bai32(PCKI, PCKI + 4096), 2 * s, None, ALU.logical_shift_right))
            op("vector", lambda E: E.tensor_scalar(bai32(ISCR, ISCR + 4096), bai32(ISCR, ISCR + 4096), 3, None, ALU.bitwise_and))
            op("vector", lambda E: E.tensor_scalar(bai32(ISCR, ISCR + 4096), bai32(ISCR, ISCR + 4096), 1, None, ALU.subtract))
            op("vector", lambda E, s=s: E.tensor_copy(ba(WQOM + 2048 * s, WQOM + 2048 * s + 2048), bai32(ISCR, ISCR + 4096)))
            unp_tok[s] = vevt()
        wqt_tr = [None] * 8
        for oc in range(8):
            wait("sync", unp_tok[oc // 2])
            out3d = ba(WQTS, WQTS + 8192).rearrange("p (c t) -> p c t", c=8)[:, :, oc * 128:(oc + 1) * 128]
            wqt_tr[oc] = op("sync", lambda E, out3d=out3d, oc=oc: E.dma_start(
                out=out3d, in_=ba(WQOM + oc * 1024, WQOM + oc * 1024 + 1024), transpose=True), inc="stw", n=16)
        wait("sync", wqt_tr[7])
        t_wqt_store = op_sd(lambda E: E.dma_start(out=wqt_dram.ap(), in_=ba(WQTS, WQTS + 8192)))

        wait("gpsimd", t_wqt_store)
        wkv_tok = [None] * 8
        for d in range(8):
            src = wqkv_d[d * 128:(d + 1) * 128, 1024:3072]
            wkv_tok[d] = op("gpsimd", lambda E, src=src, d=d: E.dma_start(
                out=ba(WKV + d * 2048, WKV + d * 2048 + 2048), in_=src), inc="gd", n=16)

        WCST = 53248
        cast_jobs = []
        for d in range(8):
            cast_jobs.append((wqkv_d[d * 128:(d + 1) * 128, 0:1024], False,
                              wq_dram.ap()[:, d * 1024:(d + 1) * 1024]))
        for hc in range(NHC):
            src_g = wg_d[:, hc * 128:(hc + 1) * 128].rearrange("(c p) h -> p c h", p=128)
            src_u = wu_d[:, hc * 128:(hc + 1) * 128].rearrange("(c p) h -> p c h", p=128)
            cast_jobs.append((src_g, True, wgu_dram.ap()[:, hc * 2048:hc * 2048 + 1024]))
            cast_jobs.append((src_u, True, wgu_dram.ap()[:, hc * 2048 + 1024:hc * 2048 + 2048]))
        for hc in range(NHC):
            cast_jobs.append((wd_d[hc * 128:(hc + 1) * 128, :], False,
                              wdb_dram.ap()[:, hc * 1024:(hc + 1) * 1024]))
        NCS = 4
        cast_ld = [None] * NCS
        cast_st = [None] * NCS
        cast_store_by_j = [None] * len(cast_jobs)

        def cast_issue_load(j):
            p = j % NCS
            slot = ba(WCST + p * 2048, WCST + p * 2048 + 1024)
            srcap, is3d, dst = cast_jobs[j]
            wait("gpsimd", cast_st[p])
            out_ap = slot.rearrange("p (c h) -> p c h", c=8) if is3d else slot
            cast_ld[p] = op("gpsimd", lambda E, out_ap=out_ap, srcap=srcap: E.dma_start(
                out=out_ap, in_=srcap), inc="cl%d" % p, n=16)

        def cast_issue_store(j):
            p = j % NCS
            slot = ba(WCST + p * 2048, WCST + p * 2048 + 1024)
            dst = cast_jobs[j][2]
            wait("gpsimd", cast_ld[p])
            cast_st[p] = op("gpsimd", lambda E, slot=slot, dst=dst: E.dma_start(
                out=dst, in_=slot), inc="cs%d" % p, n=16)
            cast_store_by_j[j] = cast_st[p]

        def emit_cast_range(j0, j1):
            for j in range(j0, j1):
                if j == 0:
                    cast_issue_load(0)
                    cast_issue_load(1)
                cast_issue_store(j)
                if j + 2 < len(cast_jobs):
                    cast_issue_load(j + 2)

        cast_split = 8 + 2 * NHC
        emit_cast_range(0, cast_split)
        wq_cast_toks = cast_store_by_j[4:8]
        gu_cast_toks = cast_store_by_j[cast_split - 4:cast_split]

        x_load = {}
        x_slot_free = [None, None]
        stg_a_free = [None, None]
        xn_tr = [None] * NTC

        def load_x(li, tc):
            slot = li % 2
            eng = "sync" if slot == 0 else "scalar"
            wait(eng, x_slot_free[slot])
            x_load[li] = op(eng, lambda E, tc=tc, slot=slot: E.dma_start(
                out=fa(XST + slot * 1024, XST + slot * 1024 + 1024),
                in_=x_d[tc * 128:(tc + 1) * 128, :]), inc="dx%d" % slot, n=16)
            return x_load[li]

        GA = 4
        kv_evt = None
        kp_last_evt = None
        kv_mm_done = {}
        _ast = {}

        def emit_a1_chunk(g, gi):
            tcs = list(range(g * GA, (g + 1) * GA))
            tc = tcs[gi]
            slot = tc % 2
            xs = fa(XST + slot * 1024, XST + slot * 1024 + 1024)
            sscr_a = ba(SSCR_A, SSCR_A + 1024).bitcast(F32)
            wait("vector", x_load[tc])
            wait("scalar", x_load[tc])
            stok_a1 = ssums(xs, SS1 + tc, QQ1 + tc, sscr_a)
            op("vector", lambda E, xs=xs, tc=tc: E.tensor_reduce(fa(MX1 + tc, MX1 + tc + 1), xs[:, 0:512], axis=mybir.AxisListType.X, op=ALU.max))
            op("vector", lambda E, xs=xs, tc=tc: E.tensor_reduce(fa(MX2 + tc, MX2 + tc + 1), xs[:, 512:1024], axis=mybir.AxisListType.X, op=ALU.max))
            op("vector", lambda E, xs=xs, tc=tc: E.tensor_reduce(fa(MN1 + tc, MN1 + tc + 1), xs[:, 0:512], axis=mybir.AxisListType.X, op=ALU.min))
            wait("vector", stok_a1)
            vt = op("vector", lambda E, xs=xs, tc=tc: E.tensor_reduce(fa(MN2 + tc, MN2 + tc + 1), xs[:, 512:1024], axis=mybir.AxisListType.X, op=ALU.min), inc="ve")
            x_slot_free[slot] = vt
            if gi + 2 < GA:
                load_x(tcs[gi + 2], tcs[gi + 2])
            _ast[g] = stok_a1

        def emit_a1(g):
            tcs = list(range(g * GA, (g + 1) * GA))
            load_x(tcs[0], tcs[0])
            load_x(tcs[1], tcs[1])
            for gi in range(GA):
                emit_a1_chunk(g, gi)

        def emit_batch(g):
            stok_a1 = _ast[g]
            wait("vector", stok_a1)
            if g == 0:
                wait("vector", t_wsb)
            off = g * GA
            ln_batch(SS1 + off, QQ1 + off, MU_A + off, TMP, R_A + off, GA, a_tail=off)


        def emit_a2kv(g):
            nonlocal kv_evt, kp_last_evt
            tcs = list(range(g * GA, (g + 1) * GA))
            if g == 0:
                for d in range(8):
                    wait("tensor", wkv_tok[d])
            for tci, tc in enumerate(tcs):
                emit_xn_chunk(tc)
                emit_kvproj_chunk(g, tci, tc)

        def emit_xn_chunk(tc):
            li = 50 + tc
            tok = load_x(li, tc)
            wait("vector", tok)
            slot = li % 2
            xs = fa(XST + slot * 1024, XST + slot * 1024 + 1024)
            sslot = tc % 2
            xstg = ba(XSTG_A + sslot * 1024, XSTG_A + sslot * 1024 + 1024)
            wait("vector", stg_a_free[sslot])
            mu = fa(MU_A + tc, MU_A + tc + 1)
            rr = fa(R_A + tc, R_A + tc + 1)
            op("vector", lambda E, xstg=xstg, xs=xs, mu=mu, rr=rr: E.tensor_scalar(xstg, xs, mu, rr, ALU.subtract, ALU.mult))
            vt = vevt()
            x_slot_free[slot] = vt
            wait("sync", vt)
            if tc == 0:
                wait("sync", t_wqt_store)
            out3d = ba(XNF, XNF + 16384).rearrange("p (c t) -> p c t", c=8)[:, :, tc * 128:(tc + 1) * 128]
            tr = op("sync", lambda E, out3d=out3d, xstg=xstg: E.dma_start(out=out3d, in_=xstg, transpose=True), inc="std%d" % sslot, n=16)
            xn_tr[tc] = tr
            stg_a_free[sslot] = tr

        kv_evts = {}

        def emit_kvproj_chunk(g, tci, tc):
            nonlocal kv_evt, kp_last_evt
            wait("tensor", xn_tr[tc])
            if g > 0 and tci in (0, 1):
                wait("tensor", kv_mm_done[(g - 1, 7)])
            prev2 = kv_evts.get(tc - 2)
            if prev2 is not None:
                wait("tensor", prev2)
            qb4 = (tc % 2) * 4
            pk0, pk1, pv0, pv1 = PS[qb4], PS[qb4 + 1], PS[qb4 + 2], PS[qb4 + 3]
            tok_kv = None
            for d in range(8):
                lhs = ba(XNF + d * 2048 + tc * 128, XNF + d * 2048 + tc * 128 + 128)
                st, sp = (d == 0), (d == 7)
                for ec, pt in enumerate([pk0, pk1, pv0, pv1]):
                    rhs = ba(WKV + d * 2048 + ec * 512, WKV + d * 2048 + ec * 512 + 512)
                    if ec == 3 and d == 7:
                        tok_kv = op("tensor", lambda E, pt=pt, lhs=lhs, rhs=rhs, st=st, sp=sp: E.matmul(
                            pt[:, :], lhs, rhs, start=st, stop=sp), inc="te")
                    else:
                        op("tensor", lambda E, pt=pt, lhs=lhs, rhs=rhs, st=st, sp=sp: E.matmul(
                            pt[:, :], lhs, rhs, start=st, stop=sp))
            wait("vector", tok_kv)
            if tc == 0:
                wait("vector", t_bkv)
                wait("scalar", t_zeros)
            evs = []
            for i, pt in enumerate([pk0, pk1]):
                kd = ba(KPH + tci * 1024 + i * 512, KPH + tci * 1024 + i * 512 + 512)
                kb = fa(BIAS + i * 512, BIAS + i * 512 + 512)
                sc1 = fa(SCR1, SCR1 + 512)
                sc2 = fa(SCR2, SCR2 + 512)
                op("vector", lambda E, sc1=sc1, pt=pt, kb=kb: E.tensor_tensor(sc1, pt[:, :], kb, ALU.add))
                vv = op("vector", lambda E, sc2=sc2, sc1=sc1: E.tensor_scalar(sc2, sc1, 0.0, None, ALU.min), inc="ve")
                wait("scalar", vv)
                stok = op("scalar", lambda E, sc2=sc2: E.activation(sc2, sc2, AF.Exp, bias=fa(ZEROS, ZEROS + 1), scale=1.0), inc="se")
                wait("vector", stok)
                evs.append(op("vector", lambda E, kd=kd, sc1=sc1, sc2=sc2: E.scalar_tensor_tensor(
                    kd, sc1, 0.0, sc2, ALU.max, ALU.add), inc="ve"))
            for i, pt in enumerate([pv0, pv1]):
                vd = ba(VH + tci * 1024 + i * 512, VH + tci * 1024 + i * 512 + 512)
                vb = fa(BIAS + 1024 + i * 512, BIAS + 1024 + i * 512 + 512)
                evs.append(op("vector", lambda E, vd=vd, pt=pt, vb=vb: E.tensor_tensor(vd, pt[:, :], vb, ALU.add), inc="ve"))
            kv_evts[tc] = evs[-1]
            kv_evt = evs[-1]
            kp_last_evt = evs[-1]

        def emit_kvmm(g, work=()):
            work = list(work)
            wait("tensor", kp_last_evt)
            for dc in range(8):
                pA = PS[4 + (dc % 2) * 2]
                pB = PS[5 + (dc % 2) * 2]
                pC = PS[dc % 2]
                if dc >= 2:
                    wait("tensor", kv_mm_done[(g, dc - 2)])
                if dc == 0 and g == 0:
                    wait("tensor", t_ones)
                mm_tok = None
                for tci in range(GA):
                    lhs = ba(KPH + tci * 1024 + dc * 128, KPH + tci * 1024 + dc * 128 + 128)
                    st, sp = (tci == 0), (tci == GA - 1)
                    op("tensor", lambda E, pA=pA, lhs=lhs, tci=tci, st=st, sp=sp: E.matmul(
                        pA[:, :], lhs, ba(VH + tci * 1024, VH + tci * 1024 + 512), start=st, stop=sp))
                    op("tensor", lambda E, pB=pB, lhs=lhs, tci=tci, st=st, sp=sp: E.matmul(
                        pB[:, :], lhs, ba(VH + tci * 1024 + 512, VH + tci * 1024 + 1024), start=st, stop=sp))
                    mm_tok = op("tensor", lambda E, pC=pC, lhs=lhs, st=st, sp=sp: E.matmul(
                        pC[:, 0:8], lhs, ba(ONESB, ONESB + 8), start=st, stop=sp), inc="te")
                wait("vector", mm_tok)
                base = RF + dc * 1032
                if g == 0:
                    op("vector", lambda E, base=base, pA=pA: E.tensor_copy(fa(base, base + 512), pA[:, :]))
                    op("vector", lambda E, base=base, pB=pB: E.tensor_copy(fa(base + 512, base + 1024), pB[:, :]))
                    op("vector", lambda E, base=base, pC=pC: E.tensor_copy(fa(base + 1024, base + 1025), pC[:, 0:1]))
                    op("vector", lambda E, base=base: E.memset(fa(base + 1025, base + 1032), 0.0))
                    ev = vevt()
                else:
                    op("vector", lambda E, base=base, pA=pA: E.tensor_tensor(fa(base, base + 512), pA[:, :], fa(base, base + 512), ALU.add))
                    op("vector", lambda E, base=base, pB=pB: E.tensor_tensor(fa(base + 512, base + 1024), pB[:, :], fa(base + 512, base + 1024), ALU.add))
                    op("vector", lambda E, base=base, pC=pC: E.tensor_tensor(fa(base + 1024, base + 1025), pC[:, 0:1], fa(base + 1024, base + 1025), ALU.add))
                    ev = vevt()
                kv_mm_done[(g, dc)] = ev
                if work:
                    work.pop(0)()

        NG = NTC // GA
        emit_a1(0)
        emit_batch(0)
        wait("tensor", wkv_tok[7])
        for tc0 in range(GA):
            emit_xn_chunk(tc0)
        for g in range(NG):
            for tci, tc in enumerate(range(g * GA, (g + 1) * GA)):
                emit_kvproj_chunk(g, tci, tc)
            if g + 1 < NG:
                gg = g + 1
                tcs2 = list(range(gg * GA, (gg + 1) * GA))
                load_x(tcs2[0], tcs2[0])
                load_x(tcs2[1], tcs2[1])
                emit_kvmm(g, [
                    lambda gg=gg: emit_a1_chunk(gg, 0),
                    lambda gg=gg: emit_a1_chunk(gg, 1),
                    lambda gg=gg: emit_a1_chunk(gg, 2),
                    lambda gg=gg: emit_a1_chunk(gg, 3),
                    lambda gg=gg: emit_batch(gg),
                    lambda t=tcs2[0]: emit_xn_chunk(t),
                    lambda t=tcs2[1]: emit_xn_chunk(t),
                    lambda t=tcs2[2]: emit_xn_chunk(t),
                ])
                emit_xn_chunk(tcs2[3])
            else:
                emit_kvmm(g)
        kv_mm_done[(1, 7)] = kv_mm_done[(NTC // GA - 1, 7)]

        dbg_gate_vec = None
        if DEBUG:
            wait("sync", kp_last_evt)
            t_d1 = op("sync", lambda E: E.dma_start(out=dbg["kpv"][:, :], in_=ba(KPH, KPH + 16384)), inc="sd", n=16)
            wait("sync", kv_mm_done[(1, 7)])
            t_d2 = op("sync", lambda E: E.dma_start(out=dbg["kvpre"][:, :], in_=fa(RF, RF + 8256)), inc="sd", n=16)
            wait("sync", xn_tr[NTC - 1])
            t_d3 = op("sync", lambda E: E.dma_start(out=dbg["xnf"][:, :], in_=ba(XNF, XNF + 16384)), inc="sd", n=16)
            dbg_gate_vec = t_d3
        wq_tok = [None] * 8
        wait("scalar", kp_last_evt)
        for t4 in wq_cast_toks:
            wait("scalar", t4)
        for d in range(8):
            wq_tok[d] = op("scalar", lambda E, d=d: E.dma_start(
                out=ba(WQ + d * 1024, WQ + d * 1024 + 1024),
                in_=wq_dram.ap()[:, d * 1024:(d + 1) * 1024]), inc="stw", n=16)

        wait("gpsimd", kv_mm_done[(1, 7)])
        t_b = op("gpsimd", lambda E: E.dma_start(out=cc_in.ap(), in_=fa(RF, RF + 8256)), inc="gd", n=16)
        wait("gpsimd", t_b)
        t_cc = op("gpsimd", lambda E: E.collective_compute(
            "AllReduce", mybir.AluOpType.add,
            replica_groups=[list(range(NCORES))],
            ins=[cc_in.ap().opt()],
            outs=[cc_out.ap().opt()]), inc="cc")
        emit_cast_range(cast_split, len(cast_jobs))
        wd_cast_toks = cast_store_by_j[-4:]

        q_evt = None
        q_evt_prev = None
        for qe in range(8):
            if qe in (0, 1):
                wait("tensor", wq_tok[7])
                wait("tensor", kv_mm_done[(1, 7)])
            pq = [PS[(qe % 2) * 4 + j] for j in range(4)]
            if q_evt_prev is not None:
                wait("tensor", q_evt_prev)
            qtok = None
            for d in range(8):
                lhs = ba(WQ + d * 1024 + qe * 128, WQ + d * 1024 + qe * 128 + 128)
                st, sp = (d == 0), (d == 7)
                for tq in range(4):
                    rhs = ba(XNF + d * 2048 + tq * 512, XNF + d * 2048 + tq * 512 + 512)
                    if d == 7 and tq == 3:
                        qtok = op("tensor", lambda E, p=pq[tq], lhs=lhs, rhs=rhs, st=st, sp=sp: E.matmul(
                            p[:, :], lhs, rhs, start=st, stop=sp), inc="te")
                    else:
                        op("tensor", lambda E, p=pq[tq], lhs=lhs, rhs=rhs, st=st, sp=sp: E.matmul(
                            p[:, :], lhs, rhs, start=st, stop=sp))
            wait("vector", qtok)
            if DEBUG and qe == 0:
                wait("vector", dbg_gate_vec)
            bqc = fa(BQ + qe, BQ + qe + 1)
            evs = []
            for tq in range(4):
                dst_c = qp_base(tq // 2) + qe * 1024 + (tq % 2) * 512
                dst = ba(dst_c, dst_c + 512)
                sc1 = fa(SCR1, SCR1 + 512)
                sc2 = fa(SCR2, SCR2 + 512)
                vv = op("vector", lambda E, sc1=sc1, p=pq[tq], bqc=bqc: E.scalar_tensor_tensor(
                    sc1, p[:, :], bqc, fa(ZEROS, ZEROS + 512), ALU.add, ALU.min), inc="ve")
                wait("scalar", vv)
                stok = op("scalar", lambda E, sc1=sc1: E.activation(sc1, sc1, AF.Exp, bias=fa(ZEROS, ZEROS + 1), scale=1.0), inc="se")
                op("vector", lambda E, sc2=sc2, p=pq[tq], bqc=bqc: E.scalar_tensor_tensor(
                    sc2, p[:, :], bqc, fa(ZEROS, ZEROS + 512), ALU.add, ALU.max))
                wait("vector", stok)
                evs.append(op("vector", lambda E, dst=dst, sc1=sc1, sc2=sc2: E.tensor_tensor(dst, sc1, sc2, ALU.add), inc="ve"))
            q_evt_prev = q_evt
            q_evt = evs[-1]

        if DEBUG:
            wait("sync", q_evt)
            t_d4 = op("sync", lambda E: E.dma_start(out=dbg["qp"][:, :], in_=ba(QP, QP + 16384)), inc="sd", n=16)
            wait("tensor", t_d4)
        load_x(100, 0)
        load_x(101, 1)
        wait("sync", t_cc)
        wait("sync", q_evt)
        t_r = op_sd(lambda E: E.dma_start(out=ba(KVB, KVB + 8256), in_=cc_out.ap()))
        t_kvbf = t_r

        wait("sync", kp_last_evt)
        t_bd = op_sd(lambda E: E.dma_start(out=fa(BIAS, BIAS + 1024), in_=bd_d[:, :]))
        t_bffn = op_sd(lambda E: E.dma_start(out=fa(BIAS + 1024, BIAS + 2048), in_=bffn_d[:, :]))

        def load_gb(pair, after_tok):
            g_src = g1_d if pair == 0 else g2_d
            b_src = b1_d if pair == 0 else b2_d
            wait("sync", after_tok)
            op_sd(lambda E, g_src=g_src: E.dma_start(out=fa(GB, GB + 1024), in_=g_src[:, :]))
            return op_sd(lambda E, b_src=b_src: E.dma_start(out=fa(GB + 1024, GB + 2048), in_=b_src[:, :]))

        wring_free = [q_evt, q_evt]
        sscr_b = ba(SSCR_B, SSCR_B + 1024).bitcast(F32)
        wdring_free = [q_evt, q_evt, q_evt, q_evt]
        xq_slot_free = [q_evt, q_evt]
        stg_b_free = [xn_tr[12], xn_tr[13], xn_tr[14], xn_tr[15]]
        out_slot_free = [None, None]
        gb_last_read = None
        prev_half_comb_last = None
        prev_half_comb_mm = None

        for half in range(2):
            qb = qp_base(half)
            hb = hact_base(half)

            if half == 1:
                load_x(200, 8)
                load_x(201, 9)
            gb_tok = load_gb(0, gb_last_read)
            xn1_tr = [None] * 8
            nrm_done = None
            prev_gread = None
            prev_gread2 = None
            ln2_tr_pend = []

            def ln2_flush_tr():
                while ln2_tr_pend:
                    tci2, vt2, xstg2, sslot2 = ln2_tr_pend.pop(0)
                    wait("scalar", vt2)
                    if half == 1 and tci2 == 0:
                        wait("scalar", prev_half_comb_mm)
                    out3d2 = ba(XN1F, XN1F + 8192).rearrange("p (c t) -> p c t", c=8)[:, :, tci2 * 128:(tci2 + 1) * 128]
                    tr = op("scalar", lambda E, out3d2=out3d2, xstg2=xstg2: E.dma_start(
                        out=out3d2, in_=xstg2, transpose=True), inc="std%d" % sslot2, n=16)
                    xn1_tr[tci2] = tr
                    stg_b_free[sslot2] = tr

            def ln2_norm(tci):
                nonlocal prev_gread, prev_gread2
                if tci == 0:
                    wait("vector", gb_tok)
                    wait("scalar", nrm_done)
                x1b = RF + tci * 1024
                sc0b = SCR1 if tci % 2 == 0 else OST
                sc0 = fa(sc0b, sc0b + 1024)
                rr = fa(R2 + tci, R2 + tci + 1)
                bn_c = fa(NW0 + 8 + tci, NW0 + 9 + tci)
                if prev_gread2 is not None:
                    wait("scalar", prev_gread2)
                if half == 1 and tci == 1:
                    wait("scalar", out_slot_free[0])
                if tci % 2 == 0 and tci > 0:
                    wait("scalar", nrm_done)
                ntk = op("scalar", lambda E, sc0=sc0, x1b=x1b, rr=rr, bn_c=bn_c: E.activation(
                    sc0, fa(x1b, x1b + 1024), AF.Identity, bias=bn_c, scale=rr), inc="se")
                ln2_flush_tr()
                wait("vector", ntk)
                sc1 = fa(SCR2, SCR2 + 1024)
                gtk = op("vector", lambda E, sc1=sc1, sc0=sc0: E.tensor_tensor(sc1, sc0, fa(GB, GB + 1024), ALU.mult), inc="ve")
                prev_gread2 = prev_gread
                prev_gread = gtk
                op("vector", lambda E, sc1=sc1, x1b=x1b: E.tensor_tensor(fa(x1b, x1b + 1024), sc1, fa(GB + 1024, GB + 2048), ALU.add))
                sslot = tci % 4
                xstg = ba(XSTG_B + sslot * 1024, XSTG_B + sslot * 1024 + 1024)
                wait("vector", stg_b_free[sslot])
                op("vector", lambda E, xstg=xstg, x1b=x1b: E.tensor_copy(xstg, fa(x1b, x1b + 1024)))
                vt = vevt()
                ln2_tr_pend.append((tci, vt, xstg, sslot))
                return vt
            gu_w_pre = [None, None]
            for pre in range(2):
                wait("sync", wring_free[pre])
                if half == 0 and pre == 0:
                    for t4 in gu_cast_toks:
                        wait("sync", t4)
                gu_w_pre[pre] = op("sync", lambda E, pre=pre: E.dma_start(
                    out=ba(WRG + pre * 2048, WRG + pre * 2048 + 2048),
                    in_=wgu_dram.ap()[:, pre * 2048:(pre + 1) * 2048]), inc="wr%d" % pre, n=16)

            x1_evt = [None] * 8
            ntoks = [None] * 8
            ln2_q = []
            ln2_norm_pending = []
            for tci in range(8):
                tc = half * 8 + tci
                li = 100 + half * 100 + tci
                slot = li % 2
                if li not in x_load:
                    load_x(li, tc)
                if tci == 0:
                    wait("tensor", t_kvbf)
                    if half == 1:
                        wait("tensor", prev_half_comb_last)
                pA, pB, pC = PS[(tci % 2) * 3], PS[(tci % 2) * 3 + 1], PS[(tci % 2) * 3 + 2]
                if tci >= 2:
                    wait("tensor", x1_evt[tci - 2])
                ntok = None
                for dc in range(8):
                    lhs = ba(qb + dc * 1024 + tci * 128, qb + dc * 1024 + tci * 128 + 128)
                    st, sp = (dc == 0), (dc == 7)
                    op("tensor", lambda E, pA=pA, lhs=lhs, dc=dc, st=st, sp=sp: E.matmul(
                        pA[:, :], lhs, ba(KVB + dc * 1032, KVB + dc * 1032 + 512), start=st, stop=sp))
                    op("tensor", lambda E, pB=pB, lhs=lhs, dc=dc, st=st, sp=sp: E.matmul(
                        pB[:, :], lhs, ba(KVB + dc * 1032 + 512, KVB + dc * 1032 + 1024), start=st, stop=sp))
                    ntok = op("tensor", lambda E, pC=pC, lhs=lhs, dc=dc, st=st, sp=sp: E.matmul(
                        pC[:, 0:8], lhs, ba(KVB + dc * 1032 + 1024, KVB + dc * 1032 + 1032), start=st, stop=sp), inc="te")
                ntoks[tci] = ntok
                wait("vector", ntok)
                wait("vector", x_load[li])
                op("vector", lambda E, pC=pC: E.tensor_scalar(fa(DEN, DEN + 1), pC[:, 0:1], ATTN_EPS, None, ALU.add))
                vdrain()
                op("vector", lambda E: E.reciprocal(fa(RD, RD + 1), fa(DEN, DEN + 1)))
                vdrain()
                x1b = RF + tci * 1024
                xsb = XST + slot * 1024
                xnt = fa(SCR2, SCR2 + 1024)
                mu_c = fa(MU_A + tc, MU_A + tc + 1)
                r_c = fa(R_A + tc, R_A + tc + 1)
                op("vector", lambda E, xnt=xnt, xsb=xsb, mu_c=mu_c, r_c=r_c: E.tensor_scalar(
                    xnt, fa(xsb, xsb + 1024), mu_c, r_c, ALU.subtract, ALU.mult))
                op("vector", lambda E, x1b=x1b, pA=pA, xnt=xnt: E.scalar_tensor_tensor(
                    fa(x1b, x1b + 512), pA[:, :], fa(RD, RD + 1), xnt[:, 0:512], ALU.mult, ALU.add))
                ev = op("vector", lambda E, x1b=x1b, pB=pB, xnt=xnt: E.scalar_tensor_tensor(
                    fa(x1b + 512, x1b + 1024), pB[:, :], fa(RD, RD + 1), xnt[:, 512:1024], ALU.mult, ALU.add), inc="ve")
                x1_evt[tci] = ev
                x_slot_free[slot] = ev
                wait("scalar", ev)
                ln2_stok = ssums(fa(x1b, x1b + 1024), BS1 + tci, BQ1 + tci, sscr_b)
                if tci >= 2 and tci % 2 == 0:
                    k0 = tci - 2
                    wait("vector", ln2_prev_stok)
                    ln_batch(BS1 + k0, BQ1 + k0, MU2 + k0, TMP, R2 + k0, 2, bneg_base=NW0 + 8 + k0)
                    nrm_done = vevt()
                    ln2_norm_pending += [k0, k0 + 1]
                if ln2_norm_pending:
                    ln2_norm(ln2_norm_pending.pop(0))
                ln2_prev_stok = ln2_stok

            dbg_x1_tok = None
            if DEBUG and half == 1:
                wait("sync", x1_evt[7])
                dbg_x1_tok = op("sync", lambda E: E.dma_start(out=dbg["x1"][:, :], in_=fa(RF, RF + 8192)), inc="sd", n=16)

            wait("vector", ln2_stok)
            ln_batch(BS1 + 6, BQ1 + 6, MU2 + 6, TMP, R2 + 6, 2, bneg_base=NW0 + 14)
            nrm_done = vevt()
            ln2_norm_pending += [6, 7]
            for _t in ln2_norm_pending:
                vt = ln2_norm(_t)
            ln2_norm_pending = []
            ln2_flush_tr()
            gb_last_read = vt
            hact_evt = [None] * NHC
            hact_evt0 = [None] * NHC
            gtok0s = [None] * NHC
            gtoks = [None] * NHC
            gu_mm_last = None

            def gu_first(hc, t_w):
                pg0, pu0 = PS[(hc % 2) * 4], PS[(hc % 2) * 4 + 2]
                wbase = WRG + (hc % 2) * 2048
                wait("tensor", t_w)
                if hc == 0:
                    for tr in xn1_tr[:4]:
                        wait("tensor", tr)
                    wait("tensor", x1_evt[6])
                if hc == 1:
                    wait("tensor", x1_evt[7])
                if hc >= 2:
                    wait("tensor", hact_evt0[hc - 2])
                tok = None
                for d in range(8):
                    st, sp = (d == 0), (d == 7)
                    lg = ba(wbase + d * 128, wbase + d * 128 + 128)
                    lu = ba(wbase + 1024 + d * 128, wbase + 1024 + d * 128 + 128)
                    r0 = ba(XN1F + d * 1024, XN1F + d * 1024 + 512)
                    op("tensor", lambda E, pg0=pg0, lg=lg, r0=r0, st=st, sp=sp: E.matmul(pg0[:, :], lg, r0, start=st, stop=sp))
                    tok = op("tensor", lambda E, pu0=pu0, lu=lu, r0=r0, st=st, sp=sp: E.matmul(pu0[:, :], lu, r0, start=st, stop=sp), inc="te")
                gtok0s[hc] = tok

            def gu_second(hc):
                nonlocal gu_mm_last
                pg1, pu1 = PS[(hc % 2) * 4 + 1], PS[(hc % 2) * 4 + 3]
                wbase = WRG + (hc % 2) * 2048
                if hc == 0:
                    for tr in xn1_tr[4:]:
                        wait("tensor", tr)
                    wait("tensor", x1_evt[7])
                if hc >= 2:
                    wait("tensor", hact_evt[hc - 2])
                tok = None
                for d in range(8):
                    st, sp = (d == 0), (d == 7)
                    lg = ba(wbase + d * 128, wbase + d * 128 + 128)
                    lu = ba(wbase + 1024 + d * 128, wbase + 1024 + d * 128 + 128)
                    r1 = ba(XN1F + d * 1024 + 512, XN1F + d * 1024 + 1024)
                    op("tensor", lambda E, pg1=pg1, lg=lg, r1=r1, st=st, sp=sp: E.matmul(pg1[:, :], lg, r1, start=st, stop=sp))
                    tok = op("tensor", lambda E, pu1=pu1, lu=lu, r1=r1, st=st, sp=sp: E.matmul(pu1[:, :], lu, r1, start=st, stop=sp), inc="te")
                gtoks[hc] = tok
                gu_mm_last = tok
                wring_free[hc % 2] = tok

            def gu_evac0(hc):
                pg0, pu0 = PS[(hc % 2) * 4], PS[(hc % 2) * 4 + 2]
                bgc = fa(BG + hc, BG + hc + 1)
                buc = fa(BU + hc, BU + hc + 1)
                sbase = SCR1 if hc % 2 == 0 else SCR2
                sgl = fa(sbase, sbase + 512)
                wait("scalar", gtok0s[hc])
                if hc >= 2:
                    wait("scalar", hact_evt0[hc - 2])
                s0 = op("scalar", lambda E, sgl=sgl, pg0=pg0, bgc=bgc: E.activation(sgl, pg0[:, :], AF.Silu, bias=bgc, scale=1.0), inc="se")
                wait("vector", s0)
                hact_evt0[hc] = op("vector", lambda E, hb=hb, hc=hc, pu0=pu0, buc=buc, sgl=sgl: E.scalar_tensor_tensor(
                    ba(hb + hc * 1024, hb + hc * 1024 + 512), pu0[:, :], buc, sgl, ALU.add, ALU.mult), inc="ve")

            def gu_evac1(hc):
                pg1, pu1 = PS[(hc % 2) * 4 + 1], PS[(hc % 2) * 4 + 3]
                bgc = fa(BG + hc, BG + hc + 1)
                buc = fa(BU + hc, BU + hc + 1)
                sbase = SCR1 if hc % 2 == 0 else SCR2
                sgr = fa(sbase + 512, sbase + 1024)
                wait("scalar", gtoks[hc])
                if hc >= 2:
                    wait("scalar", hact_evt[hc - 2])
                s1 = op("scalar", lambda E, sgr=sgr, pg1=pg1, bgc=bgc: E.activation(sgr, pg1[:, :], AF.Silu, bias=bgc, scale=1.0), inc="se")
                wait("vector", s1)
                hact_evt[hc] = op("vector", lambda E, hb=hb, hc=hc, pu1=pu1, buc=buc, sgr=sgr: E.scalar_tensor_tensor(
                    ba(hb + hc * 1024 + 512, hb + hc * 1024 + 1024), pu1[:, :], buc, sgr, ALU.add, ALU.mult), inc="ve")

            gu_first(0, gu_w_pre[0])
            gu_first(1, gu_w_pre[1])
            gu_second(0)
            gu_evac0(0)
            gu_evac0(1)
            gu_second(1)
            gu_evac1(0)
            gu_evac1(1)
            for hc in range(2, NHC):
                wslot = hc % 2
                wbase = WRG + wslot * 2048
                wait("sync", wring_free[wslot])
                t_w = op("sync", lambda E, wbase=wbase, hc=hc: E.dma_start(
                    out=ba(wbase, wbase + 2048), in_=wgu_dram.ap()[:, hc * 2048:(hc + 1) * 2048]), inc="wr%d" % wslot, n=16)
                gu_first(hc, t_w)
                gu_second(hc)
                gu_evac0(hc)
                gu_evac1(hc)
            hact_evt_prev_last = hact_evt[NHC - 1]

            if DEBUG and half == 1:
                wait("sync", xn1_tr[7])
                dbg_xn1_tok = op("sync", lambda E: E.dma_start(out=dbg["xn1"][:, :], in_=ba(XN1F, XN1F + 8192)), inc="sd", n=16)
            wait("gpsimd", gu_mm_last)
            t_wqtb = op("gpsimd", lambda E: E.dma_start(out=ba(WQTB, WQTB + 8192), in_=wqt_dram.ap()), inc="gd", n=16)

            gb_tok2 = load_gb(1, gb_last_read)
            prev_s3 = None

            def ln3_norm(tci):
                nonlocal prev_s3
                tc = half * 8 + tci
                x1b = RF + tci * 1024
                zs = fa(x1b, x1b + 1024)
                t0 = fa(SCR0, SCR0 + 1024)
                rr = fa(R2 + tci, R2 + tci + 1)
                bn_c = fa(NW0 + 8 + tci, NW0 + 9 + tci)
                if tci == 0:
                    wait("vector", gb_tok2)
                    wait("scalar", nrm3_done)
                if tci == 4:
                    wait("scalar", nrm3_done)
                if prev_s3 is not None:
                    wait("scalar", prev_s3)
                ntk = op("scalar", lambda E, t0=t0, zs=zs, rr=rr, bn_c=bn_c: E.activation(
                    t0, zs, AF.Identity, bias=bn_c, scale=rr), inc="se")
                wait("vector", ntk)
                op("vector", lambda E, t0=t0: E.tensor_tensor(t0, t0, fa(GB, GB + 1024), ALU.mult))
                prev_s3 = op("vector", lambda E, t0=t0, zs=zs: E.tensor_tensor(zs, t0, fa(GB + 1024, GB + 2048), ALU.add), inc="ve")

            comb_done = {}
            comb_mm = None
            q_tr = {}
            q_tra = {}
            q_xsb = {}

            def q_stage1a(tci):
                tc = half * 8 + tci
                li = 300 + half * 100 + tci
                slot = li % 2
                load_x(li, tc)
                wait("vector", x_load[li])
                xsb = XST + slot * 1024
                q_xsb[tci] = (xsb, slot, li)
                sc0 = fa(SCR1, SCR1 + 1024)
                mu = fa(MU_A + tc, MU_A + tc + 1)
                rs = fa(RS_A + tc, RS_A + tc + 1)
                op("vector", lambda E, sc0=sc0, xsb=xsb, mu=mu, rs=rs: E.tensor_scalar(
                    sc0, fa(xsb, xsb + 1024), mu, rs, ALU.subtract, ALU.mult))
                sslot = tci % 4
                xstg = ba(XSTG_B + sslot * 1024, XSTG_B + sslot * 1024 + 1024)
                wait("vector", stg_b_free[sslot])
                op("vector", lambda E, xstg=xstg, sc0=sc0: E.tensor_scalar(xstg, sc0, MAGIC, MAGIC, ALU.add, ALU.subtract))
                vt = vevt()
                qslot = tci % 2
                qbase = XQF + qslot * 1024
                wait("sync", vt)
                wait("sync", xq_slot_free[qslot])
                out3d = ba(qbase, qbase + 1024).rearrange("p (c t) -> p c t", c=8)
                tr = op("sync", lambda E, out3d=out3d, xstg=xstg: E.dma_start(out=out3d, in_=xstg, transpose=True), inc="stq%d" % sslot, n=16)
                stg_b_free[sslot] = tr
                q_tra[tci] = tr

            def q_stage1b(tci):
                qslot = tci % 2
                qbase = XQF + qslot * 1024
                pq0, pq1 = PS[(tci % 2) * 2], PS[(tci % 2) * 2 + 1]
                wait("tensor", q_tra[tci])
                if tci == 0:
                    wait("tensor", p_evt[(1, 0)])
                    wait("tensor", t_wqtb)
                elif tci == 1:
                    wait("tensor", p_evt[(1, 1)])
                else:
                    wait("tensor", comb_done[tci - 2])
                qmt = None
                for ic in range(8):
                    lhs = ba(qbase + ic * 128, qbase + ic * 128 + 128)
                    st, sp = (ic == 0), (ic == 7)
                    op("tensor", lambda E, pq0=pq0, lhs=lhs, ic=ic, st=st, sp=sp: E.matmul(
                        pq0[:, :], lhs, ba(WQTB + ic * 1024, WQTB + ic * 1024 + 512), start=st, stop=sp))
                    qmt = op("tensor", lambda E, pq1=pq1, lhs=lhs, ic=ic, st=st, sp=sp: E.matmul(
                        pq1[:, :], lhs, ba(WQTB + ic * 1024 + 512, WQTB + ic * 1024 + 1024), start=st, stop=sp), inc="te")
                xq_slot_free[qslot] = qmt
                q_tr[tci] = qmt
                return qmt

            q_stage1a(0)
            q_stage1a(1)

            wd_pre = [None] * 4
            for pre in range(4):
                wait("scalar", wdring_free[pre])
                if half == 0 and pre == 0:
                    for t4 in wd_cast_toks:
                        wait("scalar", t4)
                wd_pre[pre] = op("scalar", lambda E, pre=pre: E.dma_start(
                    out=ba(WRD + pre * 1024, WRD + pre * 1024 + 1024),
                    in_=wdb_dram.ap()[:, pre * 1024:(pre + 1) * 1024]), inc="wdr%d" % pre, n=16)

            p_evt = {}
            dn_prev = None
            for quad in range(2):
                if quad == 0:
                    wait("tensor", hact_evt[NHC - 1])
                dtok = None
                for hc in range(NHC):
                    wdslot = hc % 4
                    wdbase = WRD + wdslot * 1024
                    if quad == 0 and hc < 4:
                        t_wd = wd_pre[hc]
                    else:
                        wait("sync", wdring_free[wdslot])
                        if half == 0 and quad == 0 and hc == 4:
                            for t4 in wd_cast_toks:
                                wait("sync", t4)
                        t_wd = op("sync", lambda E, wdbase=wdbase, hc=hc: E.dma_start(
                            out=ba(wdbase, wdbase + 1024), in_=wdb_dram.ap()[:, hc * 1024:(hc + 1) * 1024]), inc="wdr%d" % wdslot, n=16)
                    wait("tensor", t_wd)
                    st, sp = (hc == 0), (hc == NHC - 1)
                    for ti in range(4):
                        tci = quad * 4 + ti
                        if quad == 1 and hc == 0:
                            wait("tensor", p_evt[(0, ti)])
                        lhs = ba(hb + hc * 1024 + tci * 128, hb + hc * 1024 + tci * 128 + 128)
                        op("tensor", lambda E, p=PS[ti * 2], lhs=lhs, wdbase=wdbase, st=st, sp=sp: E.matmul(
                            p[:, :], lhs, ba(wdbase, wdbase + 512), start=st, stop=sp))
                        dtok = op("tensor", lambda E, p=PS[ti * 2 + 1], lhs=lhs, wdbase=wdbase, st=st, sp=sp: E.matmul(
                            p[:, :], lhs, ba(wdbase + 512, wdbase + 1024), start=st, stop=sp), inc="te")
                    wdring_free[wdslot] = dtok
                wait("vector", dtok)
                if quad == 0:
                    wait("vector", t_bd)
                    if DEBUG and half == 1:
                        wait("vector", dbg_x1_tok)
                for ti in range(4):
                    tci = quad * 4 + ti
                    x1b = RF + tci * 1024
                    t0 = fa(SCR0, SCR0 + 1024)
                    zs = fa(x1b, x1b + 1024)
                    op("vector", lambda E, t0=t0, p=PS[ti * 2]: E.tensor_tensor(t0[:, 0:512], p[:, :], fa(BIAS, BIAS + 512), ALU.add))
                    p_evt[(quad, ti)] = op("vector", lambda E, t0=t0, p=PS[ti * 2 + 1]: E.tensor_tensor(t0[:, 512:1024], p[:, :], fa(BIAS + 512, BIAS + 1024), ALU.add), inc="ve")
                    z_evt = op("vector", lambda E, t0=t0, zs=zs: E.tensor_tensor(zs, t0, zs, ALU.add), inc="ve")
                    wait("scalar", z_evt)
                    ln3_stok = ssums(zs, BS1 + tci, BQ1 + tci, sscr_b)
                dn_prev = z_evt
                if quad == 0:
                    wait("vector", ln3_stok)
                    ln_batch(BS1, BQ1, MU2, TMP, R2, 4, bneg_base=NW0 + 8)
                    nrm3_done = vevt()
                    prev_s3 = None
                    for _t in range(4):
                        ln3_norm(_t)

            wait("vector", ln3_stok)
            ln_batch(BS1 + 4, BQ1 + 4, MU2 + 4, TMP, R2 + 4, 4, bneg_base=NW0 + 12)
            nrm3_done = vevt()

            q_stage1b(0)
            q_stage1b(1)
            for tci in range(8):
                tc = half * 8 + tci
                x1b = RF + tci * 1024
                if tci >= 4:
                    ln3_norm(tci)
                qmt = q_tr[tci]
                xsb, slot, li = q_xsb[tci]
                pq0, pq1 = PS[(tci % 2) * 2], PS[(tci % 2) * 2 + 1]
                wsr = fa(WSRAS + tc, WSRAS + tc + 1)
                gsc = fa(SCR2, SCR2 + 1024)
                wait("scalar", qmt)
                if tci == 0:
                    wait("scalar", gb_last_read)
                    if half == 1:
                        wait("scalar", prev_half_comb_last)
                else:
                    wait("scalar", comb_done[tci - 1])
                op("scalar", lambda E, gsc=gsc, pq0=pq0, wsr=wsr: E.activation(
                    gsc[:, 0:512], pq0[:, :], AF.Identity, bias=fa(ZEROS, ZEROS + 1), scale=wsr))
                ds1 = op("scalar", lambda E, gsc=gsc, pq1=pq1, wsr=wsr: E.activation(
                    gsc[:, 512:1024], pq1[:, :], AF.Identity, bias=fa(ZEROS, ZEROS + 1), scale=wsr), inc="se")
                wait("vector", ds1)
                if tci == 0:
                    wait("vector", t_bffn)
                vg = op("vector", lambda E, gsc=gsc: E.tensor_tensor(gsc, gsc, fa(BIAS + 1024, BIAS + 2048), ALU.add), inc="ve")
                wait("scalar", vg)
                sg_tok = op("scalar", lambda E, gsc=gsc: E.activation(gsc, gsc, AF.Silu, bias=fa(ZEROS, ZEROS + 1), scale=1.0), inc="se")
                oslot = tci % 2
                ob = OST + oslot * 1024
                wait("vector", sg_tok)
                wait("vector", out_slot_free[oslot])
                op("vector", lambda E, gsc=gsc, x1b=x1b: E.tensor_tensor(gsc, gsc, fa(x1b, x1b + 1024), ALU.mult))
                op("vector", lambda E, ob=ob, gsc=gsc, xsb=xsb: E.tensor_tensor(
                    fa(ob, ob + 1024), gsc, fa(xsb, xsb + 1024), ALU.add))
                cd = vevt()
                comb_done[tci] = cd
                comb_mm = q_tr[tci]
                x_slot_free[slot] = cd
                wait("sync", cd)
                stok = op("sync", lambda E, ob=ob, tc=tc: E.dma_start(
                    out=out_d[tc * 128:(tc + 1) * 128, :], in_=fa(ob, ob + 1024)), inc="do%d" % oslot, n=16)
                out_slot_free[oslot] = stok
                if tci + 2 < 8:
                    q_stage1a(tci + 2)
                    q_stage1b(tci + 2)
            gb_last_read = prev_s3
            prev_half_comb_last = comb_done[7]
            prev_half_comb_mm = comb_mm

        if DEBUG:
            wait("sync", prev_half_comb_last)
            op("sync", lambda E: E.dma_start(out=dbg["ba"][:, :], in_=ba(0, BA_COLS)), inc="sd", n=16)
            op("sync", lambda E: E.dma_start(out=dbg["fa"][:, :], in_=fa(0, FA_COLS)), inc="sd", n=16)
        for nm in sem_names:
            if nm in ("te", "ve", "se", "cc"):
                continue
            v = cnt[nm]
            if v:
                eng = "gpsimd" if nm == "gd" else "sync"
                script.append((eng, lambda E, s=sems[nm], v=v: E.wait_ge(s, v)))

        @block.sync
        def _(E):
            for eng, fn in script:
                if eng == "sync":
                    fn(E)

        @block.gpsimd
        def _(E):
            for eng, fn in script:
                if eng == "gpsimd":
                    fn(E)

        @block.tensor
        def _(E):
            for eng, fn in script:
                if eng == "tensor":
                    fn(E)

        @block.vector
        def _(E):
            for eng, fn in script:
                if eng == "vector":
                    fn(E)

        @block.scalar
        def _(E):
            for eng, fn in script:
                if eng == "scalar":
                    fn(E)

    return nc


def kernel(L0_attn_x, L0_attn_Wqkv, L0_attn_bqkv, L0_attn_Wg, L0_attn_bg,
           L0_attn_Wu, L0_attn_bu, L0_attn_Wd, L0_attn_bd, L0_attn_g1,
           L0_attn_b1, L0_attn_g2, L0_attn_b2, L0_ffn_packed_weight,
           L0_ffn_weight_scale, L0_ffn_bias):
    f32 = np.float32
    x = np.ascontiguousarray(np.asarray(L0_attn_x, dtype=f32).reshape(S_FULL, D))
    wqkv = np.ascontiguousarray(np.asarray(L0_attn_Wqkv, dtype=f32))
    wg = np.zeros((D, H_PAD), dtype=f32)
    wg[:, :2730] = np.asarray(L0_attn_Wg, dtype=f32)
    wu = np.zeros((D, H_PAD), dtype=f32)
    wu[:, :2730] = np.asarray(L0_attn_Wu, dtype=f32)
    wd = np.zeros((H_PAD, D), dtype=f32)
    wd[:2730, :] = np.asarray(L0_attn_Wd, dtype=f32)
    bqkv = np.asarray(L0_attn_bqkv, dtype=f32)
    bq_pp = np.ascontiguousarray(bqkv[0:1024].reshape(8, 128).T)
    bkv_bc = np.ascontiguousarray(np.broadcast_to(bqkv[1024:3072], (128, 2048)))
    bgv = np.zeros(H_PAD, dtype=f32); bgv[:2730] = np.asarray(L0_attn_bg, dtype=f32)
    buv = np.zeros(H_PAD, dtype=f32); buv[:2730] = np.asarray(L0_attn_bu, dtype=f32)
    bg_pp = np.ascontiguousarray(bgv.reshape(22, 128).T)
    bu_pp = np.ascontiguousarray(buv.reshape(22, 128).T)
    bd_bc = np.ascontiguousarray(np.broadcast_to(np.asarray(L0_attn_bd, dtype=f32), (128, 1024)))
    bffn_bc = np.ascontiguousarray(np.broadcast_to(np.asarray(L0_ffn_bias, dtype=f32), (128, 1024)))
    g1_bc = np.ascontiguousarray(np.broadcast_to(np.asarray(L0_attn_g1, dtype=f32), (128, 1024)))
    b1_bc = np.ascontiguousarray(np.broadcast_to(np.asarray(L0_attn_b1, dtype=f32), (128, 1024)))
    g2_bc = np.ascontiguousarray(np.broadcast_to(np.asarray(L0_attn_g2, dtype=f32), (128, 1024)))
    b2_bc = np.ascontiguousarray(np.broadcast_to(np.asarray(L0_attn_b2, dtype=f32), (128, 1024)))
    packed = np.ascontiguousarray(np.asarray(L0_ffn_packed_weight, dtype=np.int32).reshape(256, 1024))
    wscale = np.ascontiguousarray(np.broadcast_to(np.asarray(L0_ffn_weight_scale, dtype=f32).reshape(1, 1), (128, 1)))

    nc = build_nc()
    in_maps = []
    for c in range(NCORES):
        in_maps.append({
            "x": np.ascontiguousarray(x[c * T:(c + 1) * T]),
            "wqkv": wqkv, "wg": wg, "wu": wu, "wd": wd,
            "packed": packed, "wscale": wscale,
            "bq_pp": bq_pp, "bkv_bc": bkv_bc, "bg_pp": bg_pp, "bu_pp": bu_pp,
            "bd_bc": bd_bc, "bffn_bc": bffn_bc,
            "g1_bc": g1_bc, "b1_bc": b1_bc, "g2_bc": g2_bc, "b2_bc": b2_bc,
        })
    res = run_bass_kernel_spmd(nc, in_maps, core_ids=list(range(NCORES)))
    out = np.concatenate([res.results[c]["out"] for c in range(NCORES)], axis=0)
    if DEBUG:
        return out.reshape(1, S_FULL, D).astype(np.float32), res.results
    return out.reshape(1, S_FULL, D).astype(np.float32)


# version 32
# speedup vs baseline: 1.0636x; 1.0636x over previous
import numpy as np
from contextlib import ExitStack

import concourse.bass as bass
import concourse.mybir as mybir
from concourse.bass_utils import run_bass_kernel_spmd

F32 = mybir.dt.float32
BF16 = mybir.dt.bfloat16
I32 = mybir.dt.int32
AF = mybir.ActivationFunctionType
ALU = mybir.AluOpType

NCORES = 8
S_FULL = 16384
D = 1024
T = S_FULL // NCORES
NTC = T // 128
H_PAD = 2816
NHC = H_PAD // 128
LN_EPS = 1e-5
ATTN_EPS = 1e-6
MAGIC = 12582912.0
DEBUG = False

BA_COLS = 63488
WQOM = 0
PCKI = 8192
ISCR = 12288
WQTS = 16384
XNF = 0
WKV = 16384
WQ = 16384
KPH = 32768
VH = 40960
QP = 32768
XSTG_A = 49152
ONESB = 51200
KVB = 0
WRD = 8256
XQF = 12352
WQTB = 16384
XN1F = 16384
WRG = 24576
XSTG_B = 28672
SSCR_A = 51208
SSCR_B = 14408


def qp_base(half):
    return QP + (1 - half) * 8192


def hact_base(half):
    return 40960 if half == 0 else 32768


XST = 0
OST = 2048
SCR0 = 4096
SCR1 = 5120
SCR2 = 6144
BIAS = 7168
GB = 9216
ZEROS = 11264
SM = 11776
SS1 = SM
SS2 = SM + 16
QQ1 = SM + 32
QQ2 = SM + 48
MX1 = SM + 64
MX2 = SM + 80
MN1 = SM + 96
MN2 = SM + 112
MU_A = SM + 128
R_A = SM + 144
RS_A = SM + 160
WSRAS = SM + 176
ASC = SM + 192
TMP2 = SM + 208
TMP3 = SM + 224
BQ = SM + 240
BG = SM + 248
BU = SM + 270
WS = SM + 292
VW = SM + 296
NW0 = SM + 312
DEN = SM + 328
RD = SM + 329
TMP = SM + 332
BS1 = SM + 344
BQ1 = SM + 376
MU2 = SM + 408
R2 = SM + 416
RF = SM + 512
FA_COLS = RF + 8256


def build_nc():
    nc = bass.Bass("TRN2", target_bir_lowering=False, debug=False)

    x_d = nc.dram_tensor("x", [T, D], F32, kind="ExternalInput").ap()
    wqkv_d = nc.dram_tensor("wqkv", [D, 3 * D], F32, kind="ExternalInput").ap()
    wg_d = nc.dram_tensor("wg", [D, H_PAD], F32, kind="ExternalInput").ap()
    wu_d = nc.dram_tensor("wu", [D, H_PAD], F32, kind="ExternalInput").ap()
    wd_d = nc.dram_tensor("wd", [H_PAD, D], F32, kind="ExternalInput").ap()
    packed_d = nc.dram_tensor("packed", [256, 1024], I32, kind="ExternalInput").ap()
    bq_d = nc.dram_tensor("bq_pp", [128, 8], F32, kind="ExternalInput").ap()
    bkv_d = nc.dram_tensor("bkv_bc", [128, 2048], F32, kind="ExternalInput").ap()
    bg_d = nc.dram_tensor("bg_pp", [128, 22], F32, kind="ExternalInput").ap()
    bu_d = nc.dram_tensor("bu_pp", [128, 22], F32, kind="ExternalInput").ap()
    bd_d = nc.dram_tensor("bd_bc", [128, 1024], F32, kind="ExternalInput").ap()
    bffn_d = nc.dram_tensor("bffn_bc", [128, 1024], F32, kind="ExternalInput").ap()
    g1_d = nc.dram_tensor("g1_bc", [128, 1024], F32, kind="ExternalInput").ap()
    b1_d = nc.dram_tensor("b1_bc", [128, 1024], F32, kind="ExternalInput").ap()
    g2_d = nc.dram_tensor("g2_bc", [128, 1024], F32, kind="ExternalInput").ap()
    b2_d = nc.dram_tensor("b2_bc", [128, 1024], F32, kind="ExternalInput").ap()
    ws_d = nc.dram_tensor("wscale", [128, 1], F32, kind="ExternalInput").ap()
    out_d = nc.dram_tensor("out", [T, D], F32, kind="ExternalOutput").ap()

    dbg = {}
    if DEBUG:
        dbg["xnf"] = nc.dram_tensor("dbg_xnf", [128, 16384], BF16, kind="ExternalOutput").ap()
        dbg["kpv"] = nc.dram_tensor("dbg_kpv", [128, 16384], BF16, kind="ExternalOutput").ap()
        dbg["kvpre"] = nc.dram_tensor("dbg_kvpre", [128, 8256], F32, kind="ExternalOutput").ap()
        dbg["kvpost"] = nc.dram_tensor("dbg_kvpost", [128, 8256], F32, kind="ExternalOutput").ap()
        dbg["qp"] = nc.dram_tensor("dbg_qp", [128, 16384], BF16, kind="ExternalOutput").ap()
        dbg["x1"] = nc.dram_tensor("dbg_x1", [128, 8192], F32, kind="ExternalOutput").ap()
        dbg["xn1"] = nc.dram_tensor("dbg_xn1", [128, 8192], BF16, kind="ExternalOutput").ap()
        dbg["ba"] = nc.dram_tensor("dbg_ba", [128, BA_COLS], BF16, kind="ExternalOutput").ap()
        dbg["fa"] = nc.dram_tensor("dbg_fa", [128, FA_COLS], F32, kind="ExternalOutput").ap()
    cc_in = nc.dram_tensor("cc_in", [128, 8256], BF16)
    cc_out = nc.dram_tensor("cc_out", [128, 8256], BF16)
    wqt_dram = nc.dram_tensor("wqt_dram", [128, 8192], BF16)
    wgu_dram = nc.dram_tensor("wgu_dram", [128, 44 * 1024], BF16)
    wdb_dram = nc.dram_tensor("wdb_dram", [128, 22 * 1024], BF16)
    wq_dram = nc.dram_tensor("wq_dram", [128, 8192], BF16)

    with ExitStack() as ctx:
        e = ctx.enter_context
        BA = e(nc.sbuf_tensor("BA", [128, BA_COLS], BF16))
        FA = e(nc.sbuf_tensor("FA", [128, FA_COLS], F32))
        PS = [e(nc.psum_tensor("ps%d" % i, [128, 512], F32)) for i in range(8)]
        sem_names = ["sd", "gd", "te", "ve", "se", "cc", "dx0", "dx1",
                     "std0", "std1", "std2", "std3", "stq0", "stq1",
                     "stq2", "stq3", "stw", "wr0", "wr1",
                     "wdr0", "wdr1", "wdr2", "wdr3", "do0", "do1",
                     "cl0", "cl1", "cl2", "cl3", "cs0", "cs1", "cs2", "cs3"]
        sems = {n: e(nc.semaphore(n)) for n in sem_names}
        sd = sems["sd"]
        gd = sems["gd"]
        block = e(nc.Block())
        cnt = {k: 0 for k in sems}
        script = []

        def op(eng, fn, inc=None, n=1):
            tok = None
            if inc is not None:
                cnt[inc] += n
                tok = (inc, cnt[inc])

                def wrapped(E, fn=fn, inc=inc, n=n):
                    fn(E).then_inc(sems[inc], n)
                script.append((eng, wrapped))
            else:
                script.append((eng, lambda E, fn=fn: fn(E)))
            return tok

        def wait(eng, tok):
            if tok is None:
                return
            s, v = tok
            script.append((eng, lambda E, s=s, v=v: E.wait_ge(sems[s], v)))

        def op_sd(fn):
            tok = op("sync", fn, inc="sd", n=16)
            wait("sync", tok)
            return tok

        def ba(a, b):
            return BA[:, a:b]

        def fa(a, b):
            return FA[:, a:b]

        def bai32(a, b):
            return BA[:, a:b].bitcast(I32)

        def ssums(src_ap, sb_col, qb_col, scr):
            op("scalar", lambda E, src_ap=src_ap, sb_col=sb_col, scr=scr: E.activation(
                scr, src_ap[:, 0:512], AF.Copy, accum_out=fa(sb_col, sb_col + 1)))
            op("scalar", lambda E, src_ap=src_ap, sb_col=sb_col, scr=scr: E.activation(
                scr, src_ap[:, 512:1024], AF.Copy, accum_out=fa(sb_col + 16, sb_col + 17)))
            op("scalar", lambda E, src_ap=src_ap, qb_col=qb_col, scr=scr: E.activation(
                scr, src_ap[:, 0:512], AF.Square, accum_out=fa(qb_col, qb_col + 1)))
            return op("scalar", lambda E, src_ap=src_ap, qb_col=qb_col, scr=scr: E.activation(
                scr, src_ap[:, 512:1024], AF.Square, accum_out=fa(qb_col + 16, qb_col + 17)), inc="se")

        def ln_batch(sb, qb, mu_base, var_base, r_base, n, bneg_base=None, a_tail=None):
            W = 64
            mu = fa(SCR0, SCR0 + W)
            var = fa(SCR0 + 64, SCR0 + 64 + W)
            rr = fa(SCR0 + 128, SCR0 + 128 + W)
            vw = fa(SCR0 + 192, SCR0 + 192 + W)
            nw = fa(SCR0 + 256, SCR0 + 256 + W)
            t2 = fa(SCR0 + 320, SCR0 + 320 + W)
            t3 = fa(SCR0 + 384, SCR0 + 384 + W)
            mx = fa(SCR0 + 448, SCR0 + 448 + W)
            mn = fa(SCR0 + 512, SCR0 + 512 + W)
            asc_ = fa(SCR0 + 576, SCR0 + 576 + W)
            bn = fa(SCR0 + 640, SCR0 + 640 + W)
            op("vector", lambda E, a=mu[:, 0:n], b=fa(sb, sb + n), c=fa(sb + 16, sb + 16 + n): E.tensor_tensor(a, b, c, ALU.add))
            op("vector", lambda E, a=var[:, 0:n], b=fa(qb, qb + n), c=fa(qb + 16, qb + 16 + n): E.tensor_tensor(a, b, c, ALU.add))
            if a_tail is not None:
                off = a_tail
                op("vector", lambda E, a=mx[:, 0:n], b=fa(MX1 + off, MX1 + off + n), c=fa(MX2 + off, MX2 + off + n): E.tensor_tensor(a, b, c, ALU.max))
                op("vector", lambda E, a=mn[:, 0:n], b=fa(MN1 + off, MN1 + off + n), c=fa(MN2 + off, MN2 + off + n): E.tensor_tensor(a, b, c, ALU.min))
            vdrain()
            op("vector", lambda E, a=mu: E.tensor_scalar(a, a, 1.0 / 1024.0, None, ALU.mult))
            op("vector", lambda E, a=var: E.tensor_scalar(a, a, 1.0 / 1024.0, None, ALU.mult))
            op("vector", lambda E, a=t2, b=mu: E.tensor_tensor(a, b, b, ALU.mult))
            op("vector", lambda E, a=var, b=t2: E.tensor_tensor(a, a, b, ALU.subtract))
            op("vector", lambda E, a=vw, b=var: E.tensor_scalar(a, b, LN_EPS, None, ALU.add))
            op("vector", lambda E, a=rr, b=vw: E.tensor_scalar(a.bitcast(I32), b.bitcast(I32), 1, None, ALU.logical_shift_right))
            op("vector", lambda E, a=rr: E.tensor_scalar(a.bitcast(I32), a.bitcast(I32), 0x5F3759DF, None, ALU.subtract))
            op("vector", lambda E, a=rr: E.tensor_scalar(a.bitcast(I32), a.bitcast(I32), -1, None, ALU.mult))
            for _ in range(3):
                op("vector", lambda E, a=nw, b=rr: E.tensor_tensor(a, b, b, ALU.mult))
                op("vector", lambda E, a=nw, b=vw: E.tensor_tensor(a, a, b, ALU.mult))
                op("vector", lambda E, a=nw: E.tensor_scalar(a, a, -0.5, 1.5, ALU.mult, ALU.add))
                op("vector", lambda E, a=rr, b=nw: E.tensor_tensor(a, a, b, ALU.mult))
            if bneg_base is not None:
                op("vector", lambda E, a=bn, b=mu, c=rr: E.tensor_tensor(a, b, c, ALU.mult))
                op("vector", lambda E, a=bn: E.tensor_scalar(a, a, -1.0, None, ALU.mult))
            if a_tail is not None:
                op("vector", lambda E, a=mx, b=mu: E.tensor_tensor(a, a, b, ALU.subtract))
                op("vector", lambda E, a=mn, b=mu: E.tensor_tensor(a, b, a, ALU.subtract))
                op("vector", lambda E, a=t3, b=mx, c=mn: E.tensor_tensor(a, b, c, ALU.max))
                op("vector", lambda E, a=t3, b=rr: E.tensor_tensor(a, a, b, ALU.mult))
                op("vector", lambda E, a=t3: E.tensor_scalar(a, a, 1e-5, None, ALU.max))
                op("vector", lambda E, a=t2, b=t3: E.reciprocal(a, b))
                op("vector", lambda E, a=asc_, b=t2: E.tensor_scalar(a, b, 127.0, None, ALU.mult))
                op("vector", lambda E, a=t3, b=asc_: E.reciprocal(a, b))
            op("vector", lambda E, a=fa(mu_base, mu_base + n), b=mu[:, 0:n]: E.tensor_copy(a, b))
            op("vector", lambda E, a=fa(r_base, r_base + n), b=rr[:, 0:n]: E.tensor_copy(a, b))
            if bneg_base is not None:
                op("vector", lambda E, a=fa(bneg_base, bneg_base + n), b=bn[:, 0:n]: E.tensor_copy(a, b))
            if a_tail is not None:
                off = a_tail
                op("vector", lambda E, a=fa(WSRAS + off, WSRAS + off + n), b=t3[:, 0:n]: E.tensor_scalar(a, b, fa(WS, WS + 1), None, ALU.mult))
                op("vector", lambda E, a=fa(RS_A + off, RS_A + off + n), b=rr[:, 0:n], c=asc_[:, 0:n]: E.tensor_tensor(a, b, c, ALU.mult))
            vdrain()

        def vevt():
            return op("vector", lambda E: E.drain(), inc="ve")

        def vdrain():
            op("vector", lambda E: E.drain())

        def rsqrt_batch(var_ap, rr, w):
            nw_a, nw_b = w
            op("vector", lambda E, nw_a=nw_a, var_ap=var_ap: E.tensor_scalar(nw_a, var_ap, LN_EPS, None, ALU.add))
            vdrain()
            op("vector", lambda E, rr=rr, nw_a=nw_a: E.tensor_scalar(rr.bitcast(I32), nw_a.bitcast(I32), 1, None, ALU.logical_shift_right))
            vdrain()
            op("vector", lambda E, rr=rr: E.tensor_scalar(rr.bitcast(I32), rr.bitcast(I32), 0x5F3759DF, None, ALU.subtract))
            vdrain()
            op("vector", lambda E, rr=rr: E.tensor_scalar(rr.bitcast(I32), rr.bitcast(I32), -1, None, ALU.mult))
            vdrain()
            for _ in range(3):
                op("vector", lambda E, rr=rr, nw_b=nw_b: E.tensor_tensor(nw_b, rr, rr, ALU.mult))
                vdrain()
                op("vector", lambda E, nw_b=nw_b, nw_a=nw_a: E.tensor_tensor(nw_b, nw_b, nw_a, ALU.mult))
                vdrain()
                op("vector", lambda E, nw_b=nw_b: E.tensor_scalar(nw_b, nw_b, -0.5, 1.5, ALU.mult, ALU.add))
                vdrain()
                op("vector", lambda E, rr=rr, nw_b=nw_b: E.tensor_tensor(rr, rr, nw_b, ALU.mult))
                vdrain()

        def sums_emit(src, sb, qb, sqb, col):
            sq = fa(sqb, sqb + 1024)
            op("vector", lambda E, sq=sq, src=src: E.tensor_tensor(sq, src, src, ALU.mult))
            op("vector", lambda E, src=src, sb=sb, col=col: E.reduce_sum(fa(sb + col, sb + col + 1), src[:, 0:512], axis=mybir.AxisListType.X))
            op("vector", lambda E, src=src, sb=sb, col=col: E.reduce_sum(fa(sb + 16 + col, sb + 16 + col + 1), src[:, 512:1024], axis=mybir.AxisListType.X))
            op("vector", lambda E, sq=sq, qb=qb, col=col: E.reduce_sum(fa(qb + col, qb + col + 1), sq[:, 0:512], axis=mybir.AxisListType.X))
            return op("vector", lambda E, sq=sq, qb=qb, col=col: E.reduce_sum(fa(qb + 16 + col, qb + 16 + col + 1), sq[:, 512:1024], axis=mybir.AxisListType.X), inc="ve")

        def meanvar_batch(sb, qb, mu_ap, var_ap, n):
            s1 = fa(sb, sb + n); s2 = fa(sb + 16, sb + 16 + n)
            q1 = fa(qb, qb + n); q2 = fa(qb + 16, qb + 16 + n)
            t2 = fa(TMP2, TMP2 + n)
            vdrain()
            op("vector", lambda E, mu_ap=mu_ap, s1=s1, s2=s2: E.tensor_tensor(mu_ap, s1, s2, ALU.add))
            op("vector", lambda E, var_ap=var_ap, q1=q1, q2=q2: E.tensor_tensor(var_ap, q1, q2, ALU.add))
            vdrain()
            op("vector", lambda E, mu_ap=mu_ap: E.tensor_scalar(mu_ap, mu_ap, 1.0 / 1024.0, None, ALU.mult))
            op("vector", lambda E, var_ap=var_ap: E.tensor_scalar(var_ap, var_ap, 1.0 / 1024.0, None, ALU.mult))
            vdrain()
            op("vector", lambda E, t2=t2, mu_ap=mu_ap: E.tensor_tensor(t2, mu_ap, mu_ap, ALU.mult))
            vdrain()
            op("vector", lambda E, var_ap=var_ap, t2=t2: E.tensor_tensor(var_ap, var_ap, t2, ALU.subtract))
            vdrain()

        t_zeros = op("vector", lambda E: E.memset(fa(ZEROS, ZEROS + 512), 0.0), inc="ve")
        t_ones = op("vector", lambda E: E.memset(ba(ONESB, ONESB + 8), 1.0), inc="ve")

        op_sd(lambda E: E.dma_start(out=fa(BQ, BQ + 8), in_=bq_d[:, :]))
        op_sd(lambda E: E.dma_start(out=fa(BG, BG + 22), in_=bg_d[:, :]))
        op_sd(lambda E: E.dma_start(out=fa(BU, BU + 22), in_=bu_d[:, :]))
        t_bkv = op_sd(lambda E: E.dma_start(out=fa(BIAS, BIAS + 2048), in_=bkv_d[:, :]))
        t_wsb = op_sd(lambda E: E.dma_start(out=fa(WS, WS + 1), in_=ws_d[:, :]))

        op("gpsimd", lambda E: E.dma_start(out=bai32(PCKI, PCKI + 2048), in_=packed_d[0:128, :]), inc="gd", n=16)
        t_pck = op("gpsimd", lambda E: E.dma_start(out=bai32(PCKI + 2048, PCKI + 4096), in_=packed_d[128:256, :]), inc="gd", n=16)
        wait("vector", t_pck)
        unp_tok = [None] * 4
        for s in range(4):
            op("vector", lambda E, s=s: E.tensor_scalar(bai32(ISCR, ISCR + 4096), bai32(PCKI, PCKI + 4096), 2 * s, None, ALU.logical_shift_right))
            op("vector", lambda E: E.tensor_scalar(bai32(ISCR, ISCR + 4096), bai32(ISCR, ISCR + 4096), 3, None, ALU.bitwise_and))
            op("vector", lambda E: E.tensor_scalar(bai32(ISCR, ISCR + 4096), bai32(ISCR, ISCR + 4096), 1, None, ALU.subtract))
            op("vector", lambda E, s=s: E.tensor_copy(ba(WQOM + 2048 * s, WQOM + 2048 * s + 2048), bai32(ISCR, ISCR + 4096)))
            unp_tok[s] = vevt()
        wqt_tr = [None] * 8
        for oc in range(8):
            wait("sync", unp_tok[oc // 2])
            out3d = ba(WQTS, WQTS + 8192).rearrange("p (c t) -> p c t", c=8)[:, :, oc * 128:(oc + 1) * 128]
            wqt_tr[oc] = op("sync", lambda E, out3d=out3d, oc=oc: E.dma_start(
                out=out3d, in_=ba(WQOM + oc * 1024, WQOM + oc * 1024 + 1024), transpose=True), inc="stw", n=16)
        wait("sync", wqt_tr[7])
        t_wqt_store = op_sd(lambda E: E.dma_start(out=wqt_dram.ap(), in_=ba(WQTS, WQTS + 8192)))

        wait("gpsimd", t_wqt_store)
        wkv_tok = [None] * 8
        for d in range(8):
            src = wqkv_d[d * 128:(d + 1) * 128, 1024:3072]
            wkv_tok[d] = op("gpsimd", lambda E, src=src, d=d: E.dma_start(
                out=ba(WKV + d * 2048, WKV + d * 2048 + 2048), in_=src), inc="gd", n=16)

        WCST = 53248
        cast_jobs = []
        for d in range(8):
            cast_jobs.append((wqkv_d[d * 128:(d + 1) * 128, 0:1024], False,
                              wq_dram.ap()[:, d * 1024:(d + 1) * 1024]))
        for hc in range(NHC):
            src_g = wg_d[:, hc * 128:(hc + 1) * 128].rearrange("(c p) h -> p c h", p=128)
            src_u = wu_d[:, hc * 128:(hc + 1) * 128].rearrange("(c p) h -> p c h", p=128)
            cast_jobs.append((src_g, True, wgu_dram.ap()[:, hc * 2048:hc * 2048 + 1024]))
            cast_jobs.append((src_u, True, wgu_dram.ap()[:, hc * 2048 + 1024:hc * 2048 + 2048]))
        for hc in range(NHC):
            cast_jobs.append((wd_d[hc * 128:(hc + 1) * 128, :], False,
                              wdb_dram.ap()[:, hc * 1024:(hc + 1) * 1024]))
        NCS = 4
        cast_ld = [None] * NCS
        cast_st = [None] * NCS
        cast_store_by_j = [None] * len(cast_jobs)

        def cast_issue_load(j):
            p = j % NCS
            slot = ba(WCST + p * 2048, WCST + p * 2048 + 1024)
            srcap, is3d, dst = cast_jobs[j]
            wait("gpsimd", cast_st[p])
            out_ap = slot.rearrange("p (c h) -> p c h", c=8) if is3d else slot
            cast_ld[p] = op("gpsimd", lambda E, out_ap=out_ap, srcap=srcap: E.dma_start(
                out=out_ap, in_=srcap), inc="cl%d" % p, n=16)

        def cast_issue_store(j):
            p = j % NCS
            slot = ba(WCST + p * 2048, WCST + p * 2048 + 1024)
            dst = cast_jobs[j][2]
            wait("gpsimd", cast_ld[p])
            cast_st[p] = op("gpsimd", lambda E, slot=slot, dst=dst: E.dma_start(
                out=dst, in_=slot), inc="cs%d" % p, n=16)
            cast_store_by_j[j] = cast_st[p]

        def emit_cast_range(j0, j1):
            for j in range(j0, j1):
                if j == 0:
                    cast_issue_load(0)
                    cast_issue_load(1)
                cast_issue_store(j)
                if j + 2 < len(cast_jobs):
                    cast_issue_load(j + 2)

        cast_split = 8 + 2 * NHC
        emit_cast_range(0, cast_split)
        wq_cast_toks = cast_store_by_j[4:8]
        gu_cast_toks = cast_store_by_j[cast_split - 4:cast_split]

        x_load = {}
        x_slot_free = [None, None]
        stg_a_free = [None, None]
        xn_tr = [None] * NTC

        def load_x(li, tc):
            slot = li % 2
            eng = "sync" if slot == 0 else "scalar"
            wait(eng, x_slot_free[slot])
            x_load[li] = op(eng, lambda E, tc=tc, slot=slot: E.dma_start(
                out=fa(XST + slot * 1024, XST + slot * 1024 + 1024),
                in_=x_d[tc * 128:(tc + 1) * 128, :]), inc="dx%d" % slot, n=16)
            return x_load[li]

        GA = 4
        kv_evt = None
        kp_last_evt = None
        kv_mm_done = {}
        _ast = {}

        def emit_a1_chunk(g, gi):
            tcs = list(range(g * GA, (g + 1) * GA))
            tc = tcs[gi]
            slot = tc % 2
            xs = fa(XST + slot * 1024, XST + slot * 1024 + 1024)
            sscr_a = ba(SSCR_A, SSCR_A + 1024).bitcast(F32)
            wait("vector", x_load[tc])
            wait("scalar", x_load[tc])
            stok_a1 = ssums(xs, SS1 + tc, QQ1 + tc, sscr_a)
            op("vector", lambda E, xs=xs, tc=tc: E.tensor_reduce(fa(MX1 + tc, MX1 + tc + 1), xs[:, 0:512], axis=mybir.AxisListType.X, op=ALU.max))
            op("vector", lambda E, xs=xs, tc=tc: E.tensor_reduce(fa(MX2 + tc, MX2 + tc + 1), xs[:, 512:1024], axis=mybir.AxisListType.X, op=ALU.max))
            op("vector", lambda E, xs=xs, tc=tc: E.tensor_reduce(fa(MN1 + tc, MN1 + tc + 1), xs[:, 0:512], axis=mybir.AxisListType.X, op=ALU.min))
            wait("vector", stok_a1)
            vt = op("vector", lambda E, xs=xs, tc=tc: E.tensor_reduce(fa(MN2 + tc, MN2 + tc + 1), xs[:, 512:1024], axis=mybir.AxisListType.X, op=ALU.min), inc="ve")
            x_slot_free[slot] = vt
            if gi + 2 < GA:
                load_x(tcs[gi + 2], tcs[gi + 2])
            _ast[g] = stok_a1

        def emit_a1(g):
            tcs = list(range(g * GA, (g + 1) * GA))
            load_x(tcs[0], tcs[0])
            load_x(tcs[1], tcs[1])
            for gi in range(GA):
                emit_a1_chunk(g, gi)

        def emit_batch(g):
            stok_a1 = _ast[g]
            wait("vector", stok_a1)
            if g == 0:
                wait("vector", t_wsb)
            off = g * GA
            ln_batch(SS1 + off, QQ1 + off, MU_A + off, TMP, R_A + off, GA, a_tail=off)


        def emit_a2kv(g):
            nonlocal kv_evt, kp_last_evt
            tcs = list(range(g * GA, (g + 1) * GA))
            if g == 0:
                for d in range(8):
                    wait("tensor", wkv_tok[d])
            for tci, tc in enumerate(tcs):
                emit_xn_chunk(tc)
                emit_kvproj_chunk(g, tci, tc)

        def emit_xn_chunk(tc):
            li = 50 + tc
            tok = load_x(li, tc)
            wait("vector", tok)
            slot = li % 2
            xs = fa(XST + slot * 1024, XST + slot * 1024 + 1024)
            sslot = tc % 2
            xstg = ba(XSTG_A + sslot * 1024, XSTG_A + sslot * 1024 + 1024)
            wait("vector", stg_a_free[sslot])
            mu = fa(MU_A + tc, MU_A + tc + 1)
            rr = fa(R_A + tc, R_A + tc + 1)
            op("vector", lambda E, xstg=xstg, xs=xs, mu=mu, rr=rr: E.tensor_scalar(xstg, xs, mu, rr, ALU.subtract, ALU.mult))
            vt = vevt()
            x_slot_free[slot] = vt
            wait("sync", vt)
            if tc == 0:
                wait("sync", t_wqt_store)
            out3d = ba(XNF, XNF + 16384).rearrange("p (c t) -> p c t", c=8)[:, :, tc * 128:(tc + 1) * 128]
            tr = op("sync", lambda E, out3d=out3d, xstg=xstg: E.dma_start(out=out3d, in_=xstg, transpose=True), inc="std%d" % sslot, n=16)
            xn_tr[tc] = tr
            stg_a_free[sslot] = tr

        kv_evts = {}

        def emit_kvproj_chunk(g, tci, tc):
            nonlocal kv_evt, kp_last_evt
            wait("tensor", xn_tr[tc])
            if g > 0 and tci in (0, 1):
                wait("tensor", kv_mm_done[(g - 1, 7)])
            prev2 = kv_evts.get(tc - 2)
            if prev2 is not None:
                wait("tensor", prev2)
            qb4 = (tc % 2) * 4
            pk0, pk1, pv0, pv1 = PS[qb4], PS[qb4 + 1], PS[qb4 + 2], PS[qb4 + 3]
            tok_kv = None
            for d in range(8):
                lhs = ba(XNF + d * 2048 + tc * 128, XNF + d * 2048 + tc * 128 + 128)
                st, sp = (d == 0), (d == 7)
                for ec, pt in enumerate([pk0, pk1, pv0, pv1]):
                    rhs = ba(WKV + d * 2048 + ec * 512, WKV + d * 2048 + ec * 512 + 512)
                    if ec == 3 and d == 7:
                        tok_kv = op("tensor", lambda E, pt=pt, lhs=lhs, rhs=rhs, st=st, sp=sp: E.matmul(
                            pt[:, :], lhs, rhs, start=st, stop=sp), inc="te")
                    else:
                        op("tensor", lambda E, pt=pt, lhs=lhs, rhs=rhs, st=st, sp=sp: E.matmul(
                            pt[:, :], lhs, rhs, start=st, stop=sp))
            wait("vector", tok_kv)
            if tc == 0:
                wait("vector", t_bkv)
                wait("scalar", t_zeros)
            evs = []
            for i, pt in enumerate([pk0, pk1]):
                kd = ba(KPH + tci * 1024 + i * 512, KPH + tci * 1024 + i * 512 + 512)
                kb = fa(BIAS + i * 512, BIAS + i * 512 + 512)
                sc1 = fa(SCR1, SCR1 + 512)
                sc2 = fa(SCR2, SCR2 + 512)
                op("vector", lambda E, sc1=sc1, pt=pt, kb=kb: E.tensor_tensor(sc1, pt[:, :], kb, ALU.add))
                vv = op("vector", lambda E, sc2=sc2, sc1=sc1: E.tensor_scalar(sc2, sc1, 0.0, None, ALU.min), inc="ve")
                wait("scalar", vv)
                stok = op("scalar", lambda E, sc2=sc2: E.activation(sc2, sc2, AF.Exp, bias=fa(ZEROS, ZEROS + 1), scale=1.0), inc="se")
                wait("vector", stok)
                evs.append(op("vector", lambda E, kd=kd, sc1=sc1, sc2=sc2: E.scalar_tensor_tensor(
                    kd, sc1, 0.0, sc2, ALU.max, ALU.add), inc="ve"))
            for i, pt in enumerate([pv0, pv1]):
                vd = ba(VH + tci * 1024 + i * 512, VH + tci * 1024 + i * 512 + 512)
                vb = fa(BIAS + 1024 + i * 512, BIAS + 1024 + i * 512 + 512)
                evs.append(op("vector", lambda E, vd=vd, pt=pt, vb=vb: E.tensor_tensor(vd, pt[:, :], vb, ALU.add), inc="ve"))
            kv_evts[tc] = evs[-1]
            kv_evt = evs[-1]
            kp_last_evt = evs[-1]

        def emit_kvmm(g, work=()):
            work = list(work)
            wait("tensor", kp_last_evt)
            for dc in range(8):
                pA = PS[4 + (dc % 2) * 2]
                pB = PS[5 + (dc % 2) * 2]
                pC = PS[dc % 2]
                if dc >= 2:
                    wait("tensor", kv_mm_done[(g, dc - 2)])
                if dc == 0 and g == 0:
                    wait("tensor", t_ones)
                mm_tok = None
                for tci in range(GA):
                    lhs = ba(KPH + tci * 1024 + dc * 128, KPH + tci * 1024 + dc * 128 + 128)
                    st, sp = (tci == 0), (tci == GA - 1)
                    op("tensor", lambda E, pA=pA, lhs=lhs, tci=tci, st=st, sp=sp: E.matmul(
                        pA[:, :], lhs, ba(VH + tci * 1024, VH + tci * 1024 + 512), start=st, stop=sp))
                    op("tensor", lambda E, pB=pB, lhs=lhs, tci=tci, st=st, sp=sp: E.matmul(
                        pB[:, :], lhs, ba(VH + tci * 1024 + 512, VH + tci * 1024 + 1024), start=st, stop=sp))
                    mm_tok = op("tensor", lambda E, pC=pC, lhs=lhs, st=st, sp=sp: E.matmul(
                        pC[:, 0:8], lhs, ba(ONESB, ONESB + 8), start=st, stop=sp), inc="te")
                wait("vector", mm_tok)
                base = RF + dc * 1032
                if g == 0:
                    op("vector", lambda E, base=base, pA=pA: E.tensor_copy(fa(base, base + 512), pA[:, :]))
                    op("vector", lambda E, base=base, pB=pB: E.tensor_copy(fa(base + 512, base + 1024), pB[:, :]))
                    op("vector", lambda E, base=base, pC=pC: E.tensor_copy(fa(base + 1024, base + 1025), pC[:, 0:1]))
                    op("vector", lambda E, base=base: E.memset(fa(base + 1025, base + 1032), 0.0))
                    ev = vevt()
                else:
                    op("vector", lambda E, base=base, pA=pA: E.tensor_tensor(fa(base, base + 512), pA[:, :], fa(base, base + 512), ALU.add))
                    op("vector", lambda E, base=base, pB=pB: E.tensor_tensor(fa(base + 512, base + 1024), pB[:, :], fa(base + 512, base + 1024), ALU.add))
                    op("vector", lambda E, base=base, pC=pC: E.tensor_tensor(fa(base + 1024, base + 1025), pC[:, 0:1], fa(base + 1024, base + 1025), ALU.add))
                    ev = vevt()
                kv_mm_done[(g, dc)] = ev
                if work:
                    work.pop(0)()

        NG = NTC // GA
        emit_a1(0)
        emit_batch(0)
        wait("tensor", wkv_tok[7])
        for tc0 in range(GA):
            emit_xn_chunk(tc0)
        for g in range(NG):
            for tci, tc in enumerate(range(g * GA, (g + 1) * GA)):
                emit_kvproj_chunk(g, tci, tc)
            if g + 1 < NG:
                gg = g + 1
                tcs2 = list(range(gg * GA, (gg + 1) * GA))
                load_x(tcs2[0], tcs2[0])
                load_x(tcs2[1], tcs2[1])
                emit_kvmm(g, [
                    lambda gg=gg: emit_a1_chunk(gg, 0),
                    lambda gg=gg: emit_a1_chunk(gg, 1),
                    lambda gg=gg: emit_a1_chunk(gg, 2),
                    lambda gg=gg: emit_a1_chunk(gg, 3),
                    lambda gg=gg: emit_batch(gg),
                    lambda t=tcs2[0]: emit_xn_chunk(t),
                    lambda t=tcs2[1]: emit_xn_chunk(t),
                    lambda t=tcs2[2]: emit_xn_chunk(t),
                ])
                emit_xn_chunk(tcs2[3])
            else:
                emit_kvmm(g)
        kv_mm_done[(1, 7)] = kv_mm_done[(NTC // GA - 1, 7)]

        dbg_gate_vec = None
        if DEBUG:
            wait("sync", kp_last_evt)
            t_d1 = op("sync", lambda E: E.dma_start(out=dbg["kpv"][:, :], in_=ba(KPH, KPH + 16384)), inc="sd", n=16)
            wait("sync", kv_mm_done[(1, 7)])
            t_d2 = op("sync", lambda E: E.dma_start(out=dbg["kvpre"][:, :], in_=fa(RF, RF + 8256)), inc="sd", n=16)
            wait("sync", xn_tr[NTC - 1])
            t_d3 = op("sync", lambda E: E.dma_start(out=dbg["xnf"][:, :], in_=ba(XNF, XNF + 16384)), inc="sd", n=16)
            dbg_gate_vec = t_d3
        wq_tok = [None] * 8
        wait("scalar", kp_last_evt)
        for t4 in wq_cast_toks:
            wait("scalar", t4)
        for d in range(8):
            wq_tok[d] = op("scalar", lambda E, d=d: E.dma_start(
                out=ba(WQ + d * 1024, WQ + d * 1024 + 1024),
                in_=wq_dram.ap()[:, d * 1024:(d + 1) * 1024]), inc="stw", n=16)

        wait("gpsimd", kv_mm_done[(1, 7)])
        t_b = op("gpsimd", lambda E: E.dma_start(out=cc_in.ap(), in_=fa(RF, RF + 8256)), inc="gd", n=16)
        wait("gpsimd", t_b)
        t_cc = op("gpsimd", lambda E: E.collective_compute(
            "AllReduce", mybir.AluOpType.add,
            replica_groups=[list(range(NCORES))],
            ins=[cc_in.ap().opt()],
            outs=[cc_out.ap().opt()]), inc="cc")
        emit_cast_range(cast_split, len(cast_jobs))
        wd_cast_toks = cast_store_by_j[-4:]

        q_evt = None
        q_evt_prev = None
        for qe in range(8):
            if qe in (0, 1):
                wait("tensor", wq_tok[7])
                wait("tensor", kv_mm_done[(1, 7)])
            pq = [PS[(qe % 2) * 4 + j] for j in range(4)]
            if q_evt_prev is not None:
                wait("tensor", q_evt_prev)
            qtok = None
            for d in range(8):
                lhs = ba(WQ + d * 1024 + qe * 128, WQ + d * 1024 + qe * 128 + 128)
                st, sp = (d == 0), (d == 7)
                for tq in range(4):
                    rhs = ba(XNF + d * 2048 + tq * 512, XNF + d * 2048 + tq * 512 + 512)
                    if d == 7 and tq == 3:
                        qtok = op("tensor", lambda E, p=pq[tq], lhs=lhs, rhs=rhs, st=st, sp=sp: E.matmul(
                            p[:, :], lhs, rhs, start=st, stop=sp), inc="te")
                    else:
                        op("tensor", lambda E, p=pq[tq], lhs=lhs, rhs=rhs, st=st, sp=sp: E.matmul(
                            p[:, :], lhs, rhs, start=st, stop=sp))
            wait("vector", qtok)
            if DEBUG and qe == 0:
                wait("vector", dbg_gate_vec)
            bqc = fa(BQ + qe, BQ + qe + 1)
            evs = []
            for tq in range(4):
                dst_c = qp_base(tq // 2) + qe * 1024 + (tq % 2) * 512
                dst = ba(dst_c, dst_c + 512)
                sc1 = fa(SCR1, SCR1 + 512)
                sc2 = fa(SCR2, SCR2 + 512)
                vv = op("vector", lambda E, sc1=sc1, p=pq[tq], bqc=bqc: E.scalar_tensor_tensor(
                    sc1, p[:, :], bqc, fa(ZEROS, ZEROS + 512), ALU.add, ALU.min), inc="ve")
                wait("scalar", vv)
                stok = op("scalar", lambda E, sc1=sc1: E.activation(sc1, sc1, AF.Exp, bias=fa(ZEROS, ZEROS + 1), scale=1.0), inc="se")
                op("vector", lambda E, sc2=sc2, p=pq[tq], bqc=bqc: E.scalar_tensor_tensor(
                    sc2, p[:, :], bqc, fa(ZEROS, ZEROS + 512), ALU.add, ALU.max))
                wait("vector", stok)
                evs.append(op("vector", lambda E, dst=dst, sc1=sc1, sc2=sc2: E.tensor_tensor(dst, sc1, sc2, ALU.add), inc="ve"))
            q_evt_prev = q_evt
            q_evt = evs[-1]

        if DEBUG:
            wait("sync", q_evt)
            t_d4 = op("sync", lambda E: E.dma_start(out=dbg["qp"][:, :], in_=ba(QP, QP + 16384)), inc="sd", n=16)
            wait("tensor", t_d4)
        load_x(100, 0)
        load_x(101, 1)
        wait("sync", t_cc)
        wait("sync", q_evt)
        t_r = op_sd(lambda E: E.dma_start(out=ba(KVB, KVB + 8256), in_=cc_out.ap()))
        t_kvbf = t_r

        wait("sync", kp_last_evt)
        t_bd = op_sd(lambda E: E.dma_start(out=fa(BIAS, BIAS + 1024), in_=bd_d[:, :]))
        t_bffn = op_sd(lambda E: E.dma_start(out=fa(BIAS + 1024, BIAS + 2048), in_=bffn_d[:, :]))

        def load_gb(pair, after_tok):
            g_src = g1_d if pair == 0 else g2_d
            b_src = b1_d if pair == 0 else b2_d
            wait("sync", after_tok)
            op_sd(lambda E, g_src=g_src: E.dma_start(out=fa(GB, GB + 1024), in_=g_src[:, :]))
            return op_sd(lambda E, b_src=b_src: E.dma_start(out=fa(GB + 1024, GB + 2048), in_=b_src[:, :]))

        wring_free = [q_evt, q_evt]
        sscr_b = ba(SSCR_B, SSCR_B + 1024).bitcast(F32)
        wdring_free = [q_evt, q_evt, q_evt, q_evt]
        xq_slot_free = [q_evt, q_evt]
        stg_b_free = [xn_tr[12], xn_tr[13], xn_tr[14], xn_tr[15]]
        out_slot_free = [None, None]
        gb_last_read = None
        prev_half_comb_last = None
        prev_half_comb_mm = None

        for half in range(2):
            qb = qp_base(half)
            hb = hact_base(half)

            if half == 1:
                load_x(200, 8)
                load_x(201, 9)
            gb_tok = load_gb(0, gb_last_read)
            xn1_tr = [None] * 8
            nrm_done = None
            prev_gread = None
            prev_gread2 = None
            ln2_tr_pend = []

            def ln2_flush_tr():
                while ln2_tr_pend:
                    tci2, vt2, xstg2, sslot2 = ln2_tr_pend.pop(0)
                    wait("scalar", vt2)
                    if half == 1 and tci2 == 0:
                        wait("scalar", prev_half_comb_mm)
                    out3d2 = ba(XN1F, XN1F + 8192).rearrange("p (c t) -> p c t", c=8)[:, :, tci2 * 128:(tci2 + 1) * 128]
                    tr = op("scalar", lambda E, out3d2=out3d2, xstg2=xstg2: E.dma_start(
                        out=out3d2, in_=xstg2, transpose=True), inc="std%d" % sslot2, n=16)
                    xn1_tr[tci2] = tr
                    stg_b_free[sslot2] = tr

            def ln2_norm(tci):
                nonlocal prev_gread, prev_gread2
                if tci == 0:
                    wait("vector", gb_tok)
                    wait("scalar", nrm_done)
                x1b = RF + tci * 1024
                sc0b = SCR1 if tci % 2 == 0 else OST
                sc0 = fa(sc0b, sc0b + 1024)
                rr = fa(R2 + tci, R2 + tci + 1)
                bn_c = fa(NW0 + 8 + tci, NW0 + 9 + tci)
                if prev_gread2 is not None:
                    wait("scalar", prev_gread2)
                if half == 1 and tci == 1:
                    wait("scalar", out_slot_free[0])
                if tci % 2 == 0 and tci > 0:
                    wait("scalar", nrm_done)
                ntk = op("scalar", lambda E, sc0=sc0, x1b=x1b, rr=rr, bn_c=bn_c: E.activation(
                    sc0, fa(x1b, x1b + 1024), AF.Identity, bias=bn_c, scale=rr), inc="se")
                ln2_flush_tr()
                wait("vector", ntk)
                sc1 = fa(SCR2, SCR2 + 1024)
                gtk = op("vector", lambda E, sc1=sc1, sc0=sc0: E.tensor_tensor(sc1, sc0, fa(GB, GB + 1024), ALU.mult), inc="ve")
                prev_gread2 = prev_gread
                prev_gread = gtk
                op("vector", lambda E, sc1=sc1, x1b=x1b: E.tensor_tensor(fa(x1b, x1b + 1024), sc1, fa(GB + 1024, GB + 2048), ALU.add))
                sslot = tci % 4
                xstg = ba(XSTG_B + sslot * 1024, XSTG_B + sslot * 1024 + 1024)
                wait("vector", stg_b_free[sslot])
                op("vector", lambda E, xstg=xstg, x1b=x1b: E.tensor_copy(xstg, fa(x1b, x1b + 1024)))
                vt = vevt()
                ln2_tr_pend.append((tci, vt, xstg, sslot))
                return vt
            gu_w_pre = [None, None]
            for pre in range(2):
                wait("sync", wring_free[pre])
                if half == 0 and pre == 0:
                    for t4 in gu_cast_toks:
                        wait("sync", t4)
                gu_w_pre[pre] = op("sync", lambda E, pre=pre: E.dma_start(
                    out=ba(WRG + pre * 2048, WRG + pre * 2048 + 2048),
                    in_=wgu_dram.ap()[:, pre * 2048:(pre + 1) * 2048]), inc="wr%d" % pre, n=16)

            x1_evt = [None] * 8
            ntoks = [None] * 8
            ln2_q = []
            ln2_norm_pending = []
            for tci in range(8):
                tc = half * 8 + tci
                li = 100 + half * 100 + tci
                slot = li % 2
                if li not in x_load:
                    load_x(li, tc)
                if tci == 0:
                    wait("tensor", t_kvbf)
                    if half == 1:
                        wait("tensor", prev_half_comb_last)
                pA, pB, pC = PS[(tci % 2) * 3], PS[(tci % 2) * 3 + 1], PS[(tci % 2) * 3 + 2]
                if tci >= 2:
                    wait("tensor", x1_evt[tci - 2])
                ntok = None
                for dc in range(8):
                    lhs = ba(qb + dc * 1024 + tci * 128, qb + dc * 1024 + tci * 128 + 128)
                    st, sp = (dc == 0), (dc == 7)
                    op("tensor", lambda E, pA=pA, lhs=lhs, dc=dc, st=st, sp=sp: E.matmul(
                        pA[:, :], lhs, ba(KVB + dc * 1032, KVB + dc * 1032 + 512), start=st, stop=sp))
                    op("tensor", lambda E, pB=pB, lhs=lhs, dc=dc, st=st, sp=sp: E.matmul(
                        pB[:, :], lhs, ba(KVB + dc * 1032 + 512, KVB + dc * 1032 + 1024), start=st, stop=sp))
                    ntok = op("tensor", lambda E, pC=pC, lhs=lhs, dc=dc, st=st, sp=sp: E.matmul(
                        pC[:, 0:8], lhs, ba(KVB + dc * 1032 + 1024, KVB + dc * 1032 + 1032), start=st, stop=sp), inc="te")
                ntoks[tci] = ntok
                wait("vector", ntok)
                wait("vector", x_load[li])
                op("vector", lambda E, pC=pC: E.tensor_scalar(fa(DEN, DEN + 1), pC[:, 0:1], ATTN_EPS, None, ALU.add))
                vdrain()
                op("vector", lambda E: E.reciprocal(fa(RD, RD + 1), fa(DEN, DEN + 1)))
                vdrain()
                x1b = RF + tci * 1024
                xsb = XST + slot * 1024
                xnt = fa(SCR2, SCR2 + 1024)
                mu_c = fa(MU_A + tc, MU_A + tc + 1)
                r_c = fa(R_A + tc, R_A + tc + 1)
                op("vector", lambda E, xnt=xnt, xsb=xsb, mu_c=mu_c, r_c=r_c: E.tensor_scalar(
                    xnt, fa(xsb, xsb + 1024), mu_c, r_c, ALU.subtract, ALU.mult))
                op("vector", lambda E, x1b=x1b, pA=pA, xnt=xnt: E.scalar_tensor_tensor(
                    fa(x1b, x1b + 512), pA[:, :], fa(RD, RD + 1), xnt[:, 0:512], ALU.mult, ALU.add))
                ev = op("vector", lambda E, x1b=x1b, pB=pB, xnt=xnt: E.scalar_tensor_tensor(
                    fa(x1b + 512, x1b + 1024), pB[:, :], fa(RD, RD + 1), xnt[:, 512:1024], ALU.mult, ALU.add), inc="ve")
                x1_evt[tci] = ev
                x_slot_free[slot] = ev
                wait("scalar", ev)
                ln2_stok = ssums(fa(x1b, x1b + 1024), BS1 + tci, BQ1 + tci, sscr_b)
                if tci >= 2 and tci % 2 == 0:
                    k0 = tci - 2
                    wait("vector", ln2_prev_stok)
                    ln_batch(BS1 + k0, BQ1 + k0, MU2 + k0, TMP, R2 + k0, 2, bneg_base=NW0 + 8 + k0)
                    nrm_done = vevt()
                    ln2_norm_pending += [k0, k0 + 1]
                if ln2_norm_pending:
                    ln2_norm(ln2_norm_pending.pop(0))
                ln2_prev_stok = ln2_stok

            dbg_x1_tok = None
            if DEBUG and half == 1:
                wait("sync", x1_evt[7])
                dbg_x1_tok = op("sync", lambda E: E.dma_start(out=dbg["x1"][:, :], in_=fa(RF, RF + 8192)), inc="sd", n=16)

            wait("vector", ln2_stok)
            ln_batch(BS1 + 6, BQ1 + 6, MU2 + 6, TMP, R2 + 6, 2, bneg_base=NW0 + 14)
            nrm_done = vevt()
            ln2_norm_pending += [6, 7]
            for _t in ln2_norm_pending:
                vt = ln2_norm(_t)
            ln2_norm_pending = []
            ln2_flush_tr()
            gb_last_read = vt
            hact_evt = [None] * NHC
            hact_evt0 = [None] * NHC
            gu_mm_last = None
            for hc in range(NHC):
                wslot = hc % 2
                wbase = WRG + wslot * 2048
                if hc >= 2:
                    wait("sync", wring_free[wslot])
                    t_w = op("sync", lambda E, wbase=wbase, hc=hc: E.dma_start(
                        out=ba(wbase, wbase + 2048), in_=wgu_dram.ap()[:, hc * 2048:(hc + 1) * 2048]), inc="wr%d" % wslot, n=16)
                else:
                    t_w = gu_w_pre[hc]
                pg0, pg1, pu0, pu1 = [PS[(hc % 2) * 4 + j] for j in range(4)]
                wait("tensor", t_w)
                if hc == 0:
                    for tr in xn1_tr[:4]:
                        wait("tensor", tr)
                    wait("tensor", x1_evt[6])
                if hc >= 2:
                    wait("tensor", hact_evt0[hc - 2])
                gtok0 = None
                for d in range(8):
                    st, sp = (d == 0), (d == 7)
                    lg = ba(wbase + d * 128, wbase + d * 128 + 128)
                    lu = ba(wbase + 1024 + d * 128, wbase + 1024 + d * 128 + 128)
                    r0 = ba(XN1F + d * 1024, XN1F + d * 1024 + 512)
                    op("tensor", lambda E, pg0=pg0, lg=lg, r0=r0, st=st, sp=sp: E.matmul(pg0[:, :], lg, r0, start=st, stop=sp))
                    gtok0 = op("tensor", lambda E, pu0=pu0, lu=lu, r0=r0, st=st, sp=sp: E.matmul(pu0[:, :], lu, r0, start=st, stop=sp), inc="te")
                if hc == 0:
                    for tr in xn1_tr[4:]:
                        wait("tensor", tr)
                    wait("tensor", x1_evt[7])
                if hc >= 2:
                    wait("tensor", hact_evt[hc - 2])
                gtok = None
                for d in range(8):
                    st, sp = (d == 0), (d == 7)
                    lg = ba(wbase + d * 128, wbase + d * 128 + 128)
                    lu = ba(wbase + 1024 + d * 128, wbase + 1024 + d * 128 + 128)
                    r1 = ba(XN1F + d * 1024 + 512, XN1F + d * 1024 + 1024)
                    op("tensor", lambda E, pg1=pg1, lg=lg, r1=r1, st=st, sp=sp: E.matmul(pg1[:, :], lg, r1, start=st, stop=sp))
                    gtok = op("tensor", lambda E, pu1=pu1, lu=lu, r1=r1, st=st, sp=sp: E.matmul(pu1[:, :], lu, r1, start=st, stop=sp), inc="te")
                gu_mm_last = gtok
                wring_free[wslot] = gtok
                bgc = fa(BG + hc, BG + hc + 1)
                buc = fa(BU + hc, BU + hc + 1)
                sbase = SCR1 if hc % 2 == 0 else SCR2
                sgl = fa(sbase, sbase + 512)
                sgr = fa(sbase + 512, sbase + 1024)
                wait("scalar", gtok0)
                if hc >= 2:
                    wait("scalar", hact_evt0[hc - 2])
                s0 = op("scalar", lambda E, sgl=sgl, pg0=pg0, bgc=bgc: E.activation(sgl, pg0[:, :], AF.Silu, bias=bgc, scale=1.0), inc="se")
                wait("vector", s0)
                hact_evt0[hc] = op("vector", lambda E, hb=hb, hc=hc, pu0=pu0, buc=buc, sgl=sgl: E.scalar_tensor_tensor(
                    ba(hb + hc * 1024, hb + hc * 1024 + 512), pu0[:, :], buc, sgl, ALU.add, ALU.mult), inc="ve")
                wait("scalar", gtok)
                if hc >= 2:
                    wait("scalar", hact_evt[hc - 2])
                s1 = op("scalar", lambda E, sgr=sgr, pg1=pg1, bgc=bgc: E.activation(sgr, pg1[:, :], AF.Silu, bias=bgc, scale=1.0), inc="se")
                wait("vector", s1)
                hact_evt[hc] = op("vector", lambda E, hb=hb, hc=hc, pu1=pu1, buc=buc, sgr=sgr: E.scalar_tensor_tensor(
                    ba(hb + hc * 1024 + 512, hb + hc * 1024 + 1024), pu1[:, :], buc, sgr, ALU.add, ALU.mult), inc="ve")
            hact_evt_prev_last = hact_evt[NHC - 1]

            if DEBUG and half == 1:
                wait("sync", xn1_tr[7])
                dbg_xn1_tok = op("sync", lambda E: E.dma_start(out=dbg["xn1"][:, :], in_=ba(XN1F, XN1F + 8192)), inc="sd", n=16)
            wait("gpsimd", gu_mm_last)
            t_wqtb = op("gpsimd", lambda E: E.dma_start(out=ba(WQTB, WQTB + 8192), in_=wqt_dram.ap()), inc="gd", n=16)

            gb_tok2 = load_gb(1, gb_last_read)
            prev_s3 = None

            def ln3_norm(tci):
                nonlocal prev_s3
                tc = half * 8 + tci
                x1b = RF + tci * 1024
                zs = fa(x1b, x1b + 1024)
                t0 = fa(SCR0, SCR0 + 1024)
                rr = fa(R2 + tci, R2 + tci + 1)
                bn_c = fa(NW0 + 8 + tci, NW0 + 9 + tci)
                if tci == 0:
                    wait("vector", gb_tok2)
                    wait("scalar", nrm3_done)
                if tci == 4:
                    wait("scalar", nrm3_done)
                if prev_s3 is not None:
                    wait("scalar", prev_s3)
                ntk = op("scalar", lambda E, t0=t0, zs=zs, rr=rr, bn_c=bn_c: E.activation(
                    t0, zs, AF.Identity, bias=bn_c, scale=rr), inc="se")
                wait("vector", ntk)
                op("vector", lambda E, t0=t0: E.tensor_tensor(t0, t0, fa(GB, GB + 1024), ALU.mult))
                prev_s3 = op("vector", lambda E, t0=t0, zs=zs: E.tensor_tensor(zs, t0, fa(GB + 1024, GB + 2048), ALU.add), inc="ve")

            comb_done = {}
            comb_mm = None
            q_tr = {}
            q_tra = {}
            q_xsb = {}

            def q_stage1a(tci):
                tc = half * 8 + tci
                li = 300 + half * 100 + tci
                slot = li % 2
                load_x(li, tc)
                wait("vector", x_load[li])
                xsb = XST + slot * 1024
                q_xsb[tci] = (xsb, slot, li)
                sc0 = fa(SCR1, SCR1 + 1024)
                mu = fa(MU_A + tc, MU_A + tc + 1)
                rs = fa(RS_A + tc, RS_A + tc + 1)
                op("vector", lambda E, sc0=sc0, xsb=xsb, mu=mu, rs=rs: E.tensor_scalar(
                    sc0, fa(xsb, xsb + 1024), mu, rs, ALU.subtract, ALU.mult))
                sslot = tci % 4
                xstg = ba(XSTG_B + sslot * 1024, XSTG_B + sslot * 1024 + 1024)
                wait("vector", stg_b_free[sslot])
                op("vector", lambda E, xstg=xstg, sc0=sc0: E.tensor_scalar(xstg, sc0, MAGIC, MAGIC, ALU.add, ALU.subtract))
                vt = vevt()
                qslot = tci % 2
                qbase = XQF + qslot * 1024
                wait("sync", vt)
                wait("sync", xq_slot_free[qslot])
                out3d = ba(qbase, qbase + 1024).rearrange("p (c t) -> p c t", c=8)
                tr = op("sync", lambda E, out3d=out3d, xstg=xstg: E.dma_start(out=out3d, in_=xstg, transpose=True), inc="stq%d" % sslot, n=16)
                stg_b_free[sslot] = tr
                q_tra[tci] = tr

            def q_stage1b(tci):
                qslot = tci % 2
                qbase = XQF + qslot * 1024
                pq0, pq1 = PS[(tci % 3) * 2], PS[(tci % 3) * 2 + 1]
                wait("tensor", q_tra[tci])
                if tci == 0:
                    wait("tensor", p_evt[(1, 0)])
                    wait("tensor", t_wqtb)
                elif tci == 1:
                    wait("tensor", p_evt[(1, 1)])
                elif tci == 2:
                    wait("tensor", p_evt[(1, 2)])
                else:
                    wait("tensor", comb_done[tci - 3])
                qmt = None
                for ic in range(8):
                    lhs = ba(qbase + ic * 128, qbase + ic * 128 + 128)
                    st, sp = (ic == 0), (ic == 7)
                    op("tensor", lambda E, pq0=pq0, lhs=lhs, ic=ic, st=st, sp=sp: E.matmul(
                        pq0[:, :], lhs, ba(WQTB + ic * 1024, WQTB + ic * 1024 + 512), start=st, stop=sp))
                    qmt = op("tensor", lambda E, pq1=pq1, lhs=lhs, ic=ic, st=st, sp=sp: E.matmul(
                        pq1[:, :], lhs, ba(WQTB + ic * 1024 + 512, WQTB + ic * 1024 + 1024), start=st, stop=sp), inc="te")
                xq_slot_free[qslot] = qmt
                q_tr[tci] = qmt
                return qmt

            q_stage1a(0)
            q_stage1a(1)

            wd_pre = [None] * 4
            for pre in range(4):
                wait("scalar", wdring_free[pre])
                if half == 0 and pre == 0:
                    for t4 in wd_cast_toks:
                        wait("scalar", t4)
                wd_pre[pre] = op("scalar", lambda E, pre=pre: E.dma_start(
                    out=ba(WRD + pre * 1024, WRD + pre * 1024 + 1024),
                    in_=wdb_dram.ap()[:, pre * 1024:(pre + 1) * 1024]), inc="wdr%d" % pre, n=16)

            p_evt = {}
            dn_prev = None
            for quad in range(2):
                if quad == 0:
                    wait("tensor", hact_evt[NHC - 1])
                dtok = None
                for hc in range(NHC):
                    wdslot = hc % 4
                    wdbase = WRD + wdslot * 1024
                    if quad == 0 and hc < 4:
                        t_wd = wd_pre[hc]
                    else:
                        wait("sync", wdring_free[wdslot])
                        if half == 0 and quad == 0 and hc == 4:
                            for t4 in wd_cast_toks:
                                wait("sync", t4)
                        t_wd = op("sync", lambda E, wdbase=wdbase, hc=hc: E.dma_start(
                            out=ba(wdbase, wdbase + 1024), in_=wdb_dram.ap()[:, hc * 1024:(hc + 1) * 1024]), inc="wdr%d" % wdslot, n=16)
                    wait("tensor", t_wd)
                    st, sp = (hc == 0), (hc == NHC - 1)
                    for ti in range(4):
                        tci = quad * 4 + ti
                        if quad == 1 and hc == 0:
                            wait("tensor", p_evt[(0, ti)])
                        lhs = ba(hb + hc * 1024 + tci * 128, hb + hc * 1024 + tci * 128 + 128)
                        op("tensor", lambda E, p=PS[ti * 2], lhs=lhs, wdbase=wdbase, st=st, sp=sp: E.matmul(
                            p[:, :], lhs, ba(wdbase, wdbase + 512), start=st, stop=sp))
                        dtok = op("tensor", lambda E, p=PS[ti * 2 + 1], lhs=lhs, wdbase=wdbase, st=st, sp=sp: E.matmul(
                            p[:, :], lhs, ba(wdbase + 512, wdbase + 1024), start=st, stop=sp), inc="te")
                    wdring_free[wdslot] = dtok
                wait("vector", dtok)
                if quad == 0:
                    wait("vector", t_bd)
                    if DEBUG and half == 1:
                        wait("vector", dbg_x1_tok)
                for ti in range(4):
                    tci = quad * 4 + ti
                    x1b = RF + tci * 1024
                    t0 = fa(SCR0, SCR0 + 1024)
                    zs = fa(x1b, x1b + 1024)
                    op("vector", lambda E, t0=t0, p=PS[ti * 2]: E.tensor_tensor(t0[:, 0:512], p[:, :], fa(BIAS, BIAS + 512), ALU.add))
                    p_evt[(quad, ti)] = op("vector", lambda E, t0=t0, p=PS[ti * 2 + 1]: E.tensor_tensor(t0[:, 512:1024], p[:, :], fa(BIAS + 512, BIAS + 1024), ALU.add), inc="ve")
                    z_evt = op("vector", lambda E, t0=t0, zs=zs: E.tensor_tensor(zs, t0, zs, ALU.add), inc="ve")
                    wait("scalar", z_evt)
                    ln3_stok = ssums(zs, BS1 + tci, BQ1 + tci, sscr_b)
                dn_prev = z_evt
                if quad == 0:
                    wait("vector", ln3_stok)
                    ln_batch(BS1, BQ1, MU2, TMP, R2, 4, bneg_base=NW0 + 8)
                    nrm3_done = vevt()
                    prev_s3 = None
                    for _t in range(4):
                        ln3_norm(_t)

            wait("vector", ln3_stok)
            ln_batch(BS1 + 4, BQ1 + 4, MU2 + 4, TMP, R2 + 4, 4, bneg_base=NW0 + 12)
            nrm3_done = vevt()

            q_stage1b(0)
            q_stage1b(1)
            for tci in range(8):
                tc = half * 8 + tci
                x1b = RF + tci * 1024
                if tci >= 4:
                    ln3_norm(tci)
                qmt = q_tr[tci]
                xsb, slot, li = q_xsb[tci]
                pq0, pq1 = PS[(tci % 3) * 2], PS[(tci % 3) * 2 + 1]
                wsr = fa(WSRAS + tc, WSRAS + tc + 1)
                gsc = fa(SCR2, SCR2 + 1024)
                wait("scalar", qmt)
                if tci == 0:
                    wait("scalar", gb_last_read)
                    if half == 1:
                        wait("scalar", prev_half_comb_last)
                else:
                    wait("scalar", comb_done[tci - 1])
                op("scalar", lambda E, gsc=gsc, pq0=pq0, wsr=wsr: E.activation(
                    gsc[:, 0:512], pq0[:, :], AF.Identity, bias=fa(ZEROS, ZEROS + 1), scale=wsr))
                ds1 = op("scalar", lambda E, gsc=gsc, pq1=pq1, wsr=wsr: E.activation(
                    gsc[:, 512:1024], pq1[:, :], AF.Identity, bias=fa(ZEROS, ZEROS + 1), scale=wsr), inc="se")
                wait("vector", ds1)
                if tci == 0:
                    wait("vector", t_bffn)
                vg = op("vector", lambda E, gsc=gsc: E.tensor_tensor(gsc, gsc, fa(BIAS + 1024, BIAS + 2048), ALU.add), inc="ve")
                wait("scalar", vg)
                sg_tok = op("scalar", lambda E, gsc=gsc: E.activation(gsc, gsc, AF.Silu, bias=fa(ZEROS, ZEROS + 1), scale=1.0), inc="se")
                oslot = tci % 2
                ob = OST + oslot * 1024
                wait("vector", sg_tok)
                wait("vector", out_slot_free[oslot])
                op("vector", lambda E, gsc=gsc, x1b=x1b: E.tensor_tensor(gsc, gsc, fa(x1b, x1b + 1024), ALU.mult))
                op("vector", lambda E, ob=ob, gsc=gsc, xsb=xsb: E.tensor_tensor(
                    fa(ob, ob + 1024), gsc, fa(xsb, xsb + 1024), ALU.add))
                cd = vevt()
                comb_done[tci] = cd
                comb_mm = q_tr[tci]
                x_slot_free[slot] = cd
                wait("sync", cd)
                stok = op("sync", lambda E, ob=ob, tc=tc: E.dma_start(
                    out=out_d[tc * 128:(tc + 1) * 128, :], in_=fa(ob, ob + 1024)), inc="do%d" % oslot, n=16)
                out_slot_free[oslot] = stok
                if tci + 2 < 8:
                    q_stage1a(tci + 2)
                    q_stage1b(tci + 2)
            gb_last_read = prev_s3
            prev_half_comb_last = comb_done[7]
            prev_half_comb_mm = comb_mm

        if DEBUG:
            wait("sync", prev_half_comb_last)
            op("sync", lambda E: E.dma_start(out=dbg["ba"][:, :], in_=ba(0, BA_COLS)), inc="sd", n=16)
            op("sync", lambda E: E.dma_start(out=dbg["fa"][:, :], in_=fa(0, FA_COLS)), inc="sd", n=16)
        for nm in sem_names:
            if nm in ("te", "ve", "se", "cc"):
                continue
            v = cnt[nm]
            if v:
                eng = "gpsimd" if nm == "gd" else "sync"
                script.append((eng, lambda E, s=sems[nm], v=v: E.wait_ge(s, v)))

        @block.sync
        def _(E):
            for eng, fn in script:
                if eng == "sync":
                    fn(E)

        @block.gpsimd
        def _(E):
            for eng, fn in script:
                if eng == "gpsimd":
                    fn(E)

        @block.tensor
        def _(E):
            for eng, fn in script:
                if eng == "tensor":
                    fn(E)

        @block.vector
        def _(E):
            for eng, fn in script:
                if eng == "vector":
                    fn(E)

        @block.scalar
        def _(E):
            for eng, fn in script:
                if eng == "scalar":
                    fn(E)

    return nc


def kernel(L0_attn_x, L0_attn_Wqkv, L0_attn_bqkv, L0_attn_Wg, L0_attn_bg,
           L0_attn_Wu, L0_attn_bu, L0_attn_Wd, L0_attn_bd, L0_attn_g1,
           L0_attn_b1, L0_attn_g2, L0_attn_b2, L0_ffn_packed_weight,
           L0_ffn_weight_scale, L0_ffn_bias):
    f32 = np.float32
    x = np.ascontiguousarray(np.asarray(L0_attn_x, dtype=f32).reshape(S_FULL, D))
    wqkv = np.ascontiguousarray(np.asarray(L0_attn_Wqkv, dtype=f32))
    wg = np.zeros((D, H_PAD), dtype=f32)
    wg[:, :2730] = np.asarray(L0_attn_Wg, dtype=f32)
    wu = np.zeros((D, H_PAD), dtype=f32)
    wu[:, :2730] = np.asarray(L0_attn_Wu, dtype=f32)
    wd = np.zeros((H_PAD, D), dtype=f32)
    wd[:2730, :] = np.asarray(L0_attn_Wd, dtype=f32)
    bqkv = np.asarray(L0_attn_bqkv, dtype=f32)
    bq_pp = np.ascontiguousarray(bqkv[0:1024].reshape(8, 128).T)
    bkv_bc = np.ascontiguousarray(np.broadcast_to(bqkv[1024:3072], (128, 2048)))
    bgv = np.zeros(H_PAD, dtype=f32); bgv[:2730] = np.asarray(L0_attn_bg, dtype=f32)
    buv = np.zeros(H_PAD, dtype=f32); buv[:2730] = np.asarray(L0_attn_bu, dtype=f32)
    bg_pp = np.ascontiguousarray(bgv.reshape(22, 128).T)
    bu_pp = np.ascontiguousarray(buv.reshape(22, 128).T)
    bd_bc = np.ascontiguousarray(np.broadcast_to(np.asarray(L0_attn_bd, dtype=f32), (128, 1024)))
    bffn_bc = np.ascontiguousarray(np.broadcast_to(np.asarray(L0_ffn_bias, dtype=f32), (128, 1024)))
    g1_bc = np.ascontiguousarray(np.broadcast_to(np.asarray(L0_attn_g1, dtype=f32), (128, 1024)))
    b1_bc = np.ascontiguousarray(np.broadcast_to(np.asarray(L0_attn_b1, dtype=f32), (128, 1024)))
    g2_bc = np.ascontiguousarray(np.broadcast_to(np.asarray(L0_attn_g2, dtype=f32), (128, 1024)))
    b2_bc = np.ascontiguousarray(np.broadcast_to(np.asarray(L0_attn_b2, dtype=f32), (128, 1024)))
    packed = np.ascontiguousarray(np.asarray(L0_ffn_packed_weight, dtype=np.int32).reshape(256, 1024))
    wscale = np.ascontiguousarray(np.broadcast_to(np.asarray(L0_ffn_weight_scale, dtype=f32).reshape(1, 1), (128, 1)))

    nc = build_nc()
    in_maps = []
    for c in range(NCORES):
        in_maps.append({
            "x": np.ascontiguousarray(x[c * T:(c + 1) * T]),
            "wqkv": wqkv, "wg": wg, "wu": wu, "wd": wd,
            "packed": packed, "wscale": wscale,
            "bq_pp": bq_pp, "bkv_bc": bkv_bc, "bg_pp": bg_pp, "bu_pp": bu_pp,
            "bd_bc": bd_bc, "bffn_bc": bffn_bc,
            "g1_bc": g1_bc, "b1_bc": b1_bc, "g2_bc": g2_bc, "b2_bc": b2_bc,
        })
    res = run_bass_kernel_spmd(nc, in_maps, core_ids=list(range(NCORES)))
    out = np.concatenate([res.results[c]["out"] for c in range(NCORES)], axis=0)
    if DEBUG:
        return out.reshape(1, S_FULL, D).astype(np.float32), res.results
    return out.reshape(1, S_FULL, D).astype(np.float32)
